# version 36
# baseline (speedup 1.0000x reference)
import contextlib
import numpy as np
import concourse.bass as bass
import concourse.mybir as mybir
from concourse.bass_utils import run_bass_kernel_spmd

F32 = mybir.dt.float32
BF16 = mybir.dt.bfloat16
FP8 = mybir.dt.float8e4
Alu = mybir.AluOpType
Act = mybir.ActivationFunctionType
AX = mybir.AxisListType

T = 2048
D = 1024
TT = 256
NTAU = T // TT
NI = TT // 128
KC = 8
E_LN = 1e-5
E_ATT = 1e-6
N_LD = 8
N_TR = 10
N_ST = 4


class Ctx:
    def __init__(self):
        self.cnt = {}
        self.progs = {"sync": [], "vector": [], "scalar": [], "tensor": [], "gpsimd": []}

    def op(self, engine, fn, waits=(), inc=None, drain=False):
        after = None
        if inc is not None:
            sem, amt = inc
            self.cnt[sem] = self.cnt.get(sem, 0) + amt
            after = self.cnt[sem]
        waits = tuple((s, v) for (s, v) in waits if s is not None and v > 0)

        def emit(eng, sems, fn=fn, waits=waits, inc=inc, drain=drain):
            for s, v in waits:
                eng.wait_ge(sems[s], v)
            instr = fn(eng)
            if inc is not None:
                if instr is None:
                    instr = eng.nop()
                instr.then_inc(sems[inc[0]], inc[1])
            if drain:
                eng.drain()
        self.progs[engine].append(emit)
        return after


class DmaPool:
    def __init__(self, ctx, name, nslots):
        self.ctx, self.name, self.n = ctx, name, nslots
        self.uses = [0] * nslots
        self.rr = 0

    def dma(self, queue, fn, waits=()):
        s = self.rr
        self.rr = (self.rr + 1) % self.n
        semname = "%s%d" % (self.name, s)
        w = list(waits)
        if self.uses[s] > 0:
            w.append((semname, 16 * self.uses[s]))
        self.uses[s] += 1
        val = self.ctx.op(queue, fn, waits=w, inc=(semname, 16))
        return (semname, val)


NONE_W = ("sv", 0)


def build_nc():
    nc = bass.Bass("TRN2", target_bir_lowering=False)
    x_h = nc.dram_tensor("x", [T, D], F32, kind="ExternalInput").ap()
    W_h = nc.dram_tensor("W", [D, 2 * D], F32, kind="ExternalInput").ap()
    b_h = nc.dram_tensor("b", [2 * D], F32, kind="ExternalInput").ap()
    gamma_h = nc.dram_tensor("gamma", [D], F32, kind="ExternalInput").ap()
    beta_h = nc.dram_tensor("beta", [D], F32, kind="ExternalInput").ap()
    wf_h = nc.dram_tensor("wf", [D, D], F32, kind="ExternalInput").ap()
    bf_h = nc.dram_tensor("bf", [D], F32, kind="ExternalInput").ap()
    iden_h = nc.dram_tensor("iden", [128, 128], F32, kind="ExternalInput").ap()
    out_h = nc.dram_tensor("out", [T, D], F32, kind="ExternalOutput").ap()

    ctx = Ctx()
    C = ctx.op
    LD = DmaPool(ctx, "ld", N_LD)
    TR = DmaPool(ctx, "tr", N_TR)
    STP = DmaPool(ctx, "st", N_ST)

    es = contextlib.ExitStack()
    with es:
        Wb = es.enter_context(nc.sbuf_tensor("Wb", [128, KC * 2048], BF16))
        wdT = es.enter_context(nc.sbuf_tensor("wdT", [128, KC * 1024], BF16))
        gb = es.enter_context(nc.sbuf_tensor("gb", [128, D], F32))
        bb = es.enter_context(nc.sbuf_tensor("bb", [128, D], F32))
        ident = es.enter_context(nc.sbuf_tensor("ident", [128, 128], F32))
        identb = es.enter_context(nc.sbuf_tensor("identb", [128, 128], BF16))
        ones1 = es.enter_context(nc.sbuf_tensor("ones1", [1, 128], F32))
        grow = es.enter_context(nc.sbuf_tensor("grow", [1, 2 * D], F32))
        bfrow = es.enter_context(nc.sbuf_tensor("bfrow", [1, D], F32))
        bfh = es.enter_context(nc.sbuf_tensor("bfh", [1, D], F32))
        bvec = es.enter_context(nc.sbuf_tensor("bvec", [128, 16], F32))
        negb = es.enter_context(nc.sbuf_tensor("negb", [128, 16], F32))
        b05 = es.enter_context(nc.sbuf_tensor("b05", [128, 16], F32))
        epsln = es.enter_context(nc.sbuf_tensor("epsln", [128, 1], F32))
        epsat = es.enter_context(nc.sbuf_tensor("epsat", [128, 1], F32))
        Wst = es.enter_context(nc.sbuf_tensor("Wst", [128, 2048], F32))
        wfst = es.enter_context(nc.sbuf_tensor("wfst", [128, 1024], F32))
        wq8 = es.enter_context(nc.sbuf_tensor("wq8", [128, 1024], FP8))
        wdtok = es.enter_context(nc.sbuf_tensor("wdtok", [128, 1024], BF16))
        wsc = es.enter_context(nc.sbuf_tensor("wsc", [128, 128], F32))
        xt = es.enter_context(nc.sbuf_tensor("xt", [128, NI * D], F32))
        xn1 = es.enter_context(nc.sbuf_tensor("xn1", [128, 2 * D], F32))
        xn2 = es.enter_context(nc.sbuf_tensor("xn2", [128, 2 * D], F32))
        q81 = es.enter_context(nc.sbuf_tensor("q81", [128, 2 * D], FP8))
        q82 = q81
        xdt1 = es.enter_context(nc.sbuf_tensor("xdt1", [128, 2 * D], BF16))
        xdt2 = es.enter_context(nc.sbuf_tensor("xdt2", [128, 2 * D], BF16))
        xdf1 = es.enter_context(nc.sbuf_tensor("xdf1", [128, KC * TT], BF16))
        xdf2 = es.enter_context(nc.sbuf_tensor("xdf2", [128, KC * TT], BF16))
        yt = es.enter_context(nc.sbuf_tensor("yt", [128, NI * D], F32))
        y2t = Wst[:, 0:2 * D]
        t1t = wfst[:, 0:D]
        bby = es.enter_context(nc.sbuf_tensor("bby", [128, 2 * NI * D], F32))
        l2b = es.enter_context(nc.sbuf_tensor("l2b", [128, 2 * 2 * D], BF16))
        l2f = es.enter_context(nc.sbuf_tensor("l2f", [128, KC * TT], BF16))
        cfull = es.enter_context(nc.sbuf_tensor("cfull", [128, KC * TT], F32))
        vfull = es.enter_context(nc.sbuf_tensor("vfull", [128, KC * TT], F32))
        gsc = es.enter_context(nc.sbuf_tensor("gsc", [128, 2 * TT], F32))
        sgf = es.enter_context(nc.sbuf_tensor("sgf", [128, 2 * TT], F32))
        zsc = wdtok[:].bitcast(F32)
        hf = es.enter_context(nc.sbuf_tensor("hf", [128, KC * TT], F32))
        carry = es.enter_context(nc.sbuf_tensor("carry", [128, KC], F32))
        nt = es.enter_context(nc.sbuf_tensor("nt", [128, D], F32))
        outt = es.enter_context(nc.sbuf_tensor("outt", [128, D], F32))
        bns = es.enter_context(nc.sbuf_tensor("bns", [128, 72], F32))
        mv = es.enter_context(nc.sbuf_tensor("mv", [128, 12], F32))
        sden = es.enter_context(nc.sbuf_tensor("sden", [128, 6], F32))
        rr = es.enter_context(nc.sbuf_tensor("rr", [128, 6], F32))
        mb = es.enter_context(nc.sbuf_tensor("mb", [128, 6], F32))
        qsc = es.enter_context(nc.sbuf_tensor("qsc", [128, 512], F32))
        dmy = es.enter_context(nc.sbuf_tensor("dmy", [128, 1], F32))
        psF = es.enter_context(nc.psum_tensor("psF", [128, 1024], F32))
        psG = es.enter_context(nc.psum_tensor("psG", [128, 1024], F32))
        psE = es.enter_context(nc.psum_tensor("psE", [128, 1024], F32))
        psP0 = es.enter_context(nc.psum_tensor("psP0", [128, 512], F32))
        psP1 = es.enter_context(nc.psum_tensor("psP1", [128, 512], F32))
        sems = {}
        for nm in (["sv", "ss", "sten", "sg"]
                   + ["ld%d" % i for i in range(N_LD)]
                   + ["tr%d" % i for i in range(N_TR)]
                   + ["st%d" % i for i in range(N_ST)]):
            sems[nm] = es.enter_context(nc.semaphore(nm))
        block = es.enter_context(nc.Block())
        psP = [psP0, psP1]

        ld_iden = LD.dma("sync", lambda e: e.dma_start(out=ident[:], in_=iden_h))
        ld_grow = LD.dma("sync", lambda e: e.dma_start(out=grow[:, 0:D], in_=gamma_h.unsqueeze(0)))
        ld_brow = LD.dma("sync", lambda e: e.dma_start(out=grow[:, D:2 * D], in_=beta_h.unsqueeze(0)))
        ld_bfrow = LD.dma("sync", lambda e: e.dma_start(out=bfrow[:], in_=bf_h.unsqueeze(0)))
        ld_bvec = LD.dma("sync", lambda e: e.dma_start(
            out=bvec[:], in_=b_h.rearrange("(j p) -> p j", p=128),
            allow_slow_non_contiguous=True))

        C("vector", lambda e: e.memset(epsln[:], E_LN))
        C("vector", lambda e: e.memset(epsat[:], E_ATT))
        C("vector", lambda e: e.memset(ones1[:], 1.0))
        sv_ones = C("vector", lambda e: e.tensor_copy(out=identb[:], in_=ident[:]),
                    waits=[ld_iden], inc=("sv", 1))

        prev_q = 0
        prev_wtr = NONE_W
        wfq_done = {}
        init_ld_x0 = None
        for k in range(KC):
            if k == 2:
                init_ld_x0 = LD.dma("scalar", lambda e: e.dma_start(
                    out=xt[:].rearrange("p (i c) -> p i c", i=NI),
                    in_=x_h[0:TT, :].rearrange("(i p) c -> p i c", p=128)))
            wstg = wfst[:, 0:D] if k % 2 == 0 else xn2[:, 0:D]
            ldwf = LD.dma("scalar", lambda e, k=k, wstg=wstg: e.dma_start(
                out=wstg, in_=wf_h[k * 128:(k + 1) * 128, :]),
                waits=[("sv", wfq_done.get(k - 2, 0))])
            wam, wscl, wi6, ws2 = (wsc[:, 0:32], wsc[:, 32:64], wsc[:, 64:96], wsc[:, 96:128])
            C("vector", lambda e, wstg=wstg: e.tensor_reduce(
                out=wam, in_=wstg.rearrange("p (b k) -> p b k", k=32),
                axis=AX.X, op=Alu.max, apply_absolute_value=True),
              waits=[ldwf], drain=True)
            C("vector", lambda e: e.tensor_scalar(
                out=wscl, in0=wam, scalar1=1.0 / 7.5, scalar2=1e-30,
                op0=Alu.mult, op1=Alu.max), drain=True)
            C("vector", lambda e: e.reciprocal(out=wi6, in_=wscl), drain=True)
            C("vector", lambda e: e.tensor_scalar(
                out=wi6, in0=wi6, scalar1=2.0 ** -6, scalar2=None, op0=Alu.mult), drain=True)
            C("vector", lambda e: e.tensor_scalar(
                out=ws2, in0=wscl, scalar1=32.0, scalar2=None, op0=Alu.mult), drain=True)
            wfq_done[k] = C("vector", lambda e, wstg=wstg: e.tensor_tensor(
                out=wq8[:].rearrange("p (b k) -> p b k", k=32),
                in0=wstg.rearrange("p (b k) -> p b k", k=32),
                in1=wi6.unsqueeze(-1).broadcast_to([128, 32, 32]), op=Alu.mult),
                inc=("sv", 1), drain=True)
            prev_q = C("vector", lambda e: e.tensor_tensor(
                out=wdtok[:].rearrange("p (b k) -> p b k", k=32),
                in0=wq8[:].rearrange("p (b k) -> p b k", k=32),
                in1=ws2.unsqueeze(-1).broadcast_to([128, 32, 32]), op=Alu.mult),
                waits=[prev_wtr], inc=("sv", 1), drain=True)
            prev_wtr = TR.dma("sync", lambda e, k=k: e.dma_start(
                out=wdT[:].rearrange("p (m c l) -> p m c l", m=KC, c=KC)[:, :, k, :],
                in_=wdtok[:], transpose=True),
                waits=[("sv", prev_q)])
        tr_wdT = prev_wtr

        C("vector", lambda e: e.tensor_scalar(out=negb[:], in0=bvec[:], scalar1=-1.0,
                                              scalar2=None, op0=Alu.mult),
          waits=[ld_bvec])
        sv_bvecs = C("vector", lambda e: e.tensor_scalar(out=b05[:], in0=bvec[:], scalar1=0.5,
                                                         scalar2=None, op0=Alu.add),
                     inc=("sv", 1))
        sv_bfh = C("vector", lambda e: e.tensor_scalar(out=bfh[:], in0=bfrow[:], scalar1=0.5,
                                                       scalar2=None, op0=Alu.mult),
                   waits=[ld_bfrow], inc=("sv", 1), drain=True)

        st_gbmm = 0
        for half in range(2):
            C("tensor", lambda e, h=half: e.matmul(
                out=psF[:, h * 512:(h + 1) * 512], lhsT=ones1[:],
                rhs=grow[:, h * 512:(h + 1) * 512], start=True, stop=True),
              waits=([ld_grow, ld_iden, ld_brow, ("sv", sv_ones)] if half == 0 else ()))
            st_gbmm = C("tensor", lambda e, h=half: e.matmul(
                out=psG[:, h * 512:(h + 1) * 512], lhsT=ones1[:],
                rhs=grow[:, D + h * 512:D + (h + 1) * 512], start=True, stop=True),
                inc=(("sten", 1) if half == 1 else None)) or st_gbmm
        C("vector", lambda e: e.tensor_copy(out=gb[:], in_=psF[:]),
          waits=[("sten", st_gbmm)])
        sv_gb = C("vector", lambda e: e.tensor_copy(out=bb[:], in_=psG[:]),
                  inc=("sv", 1), drain=True)

        def ln_stats2(src2_getter, base_slot, eps_tile, wait_list):
            s6a = bns[:, base_slot * 12:base_slot * 12 + 12]
            s6b = bns[:, (base_slot + 1) * 12:(base_slot + 1) * 12 + 12]
            C("vector", lambda e: e.bn_stats(out=s6a[:, 0:6], in_=src2_getter()[:, 0:512]),
              waits=wait_list)
            C("vector", lambda e: e.bn_stats(out=s6a[:, 6:12], in_=src2_getter()[:, 512:1024]))
            C("vector", lambda e: e.bn_stats(out=s6b[:, 0:6], in_=src2_getter()[:, 1024:1536]))
            C("vector", lambda e: e.bn_stats(out=s6b[:, 6:12], in_=src2_getter()[:, 1536:2048]),
              drain=True)
            mv2 = mv[:, base_slot * 2:base_slot * 2 + 4]
            C("vector", lambda e: e.bn_aggr(out=mv2[:, 0:2], in_=s6a))
            sv_mv = C("vector", lambda e: e.bn_aggr(out=mv2[:, 2:4], in_=s6b),
                      inc=("sv", 1))
            sd = sden[:, base_slot:base_slot + 2]
            varap = mv2.rearrange("p (s c) -> p s c", c=2)[:, :, 1:2].squeeze(-1)
            ss_sq = C("scalar", lambda e: e.activation(
                out=sd, in_=varap, func=Act.Sqrt, bias=eps_tile[:], scale=1.0),
                waits=[("sv", sv_mv)], inc=("ss", 1))
            r2ap = rr[:, base_slot:base_slot + 2]
            sv_r = C("vector", lambda e: e.reciprocal(out=r2ap, in_=sd),
                     waits=[("ss", ss_sq)], inc=("sv", 1), drain=True)
            rs = [rr[:, base_slot + i:base_slot + 1 + i] for i in range(2)]
            mus = [mv[:, base_slot * 2 + 2 * i:base_slot * 2 + 2 * i + 1] for i in range(2)]
            return sv_r, rs, mus

        def mkmb(base_slot, sv_r):
            mv2mu = mv[:, base_slot * 2:base_slot * 2 + 4].rearrange(
                "p (s c) -> p s c", c=2)[:, :, 0:1].squeeze(-1)
            m2ap = mb[:, base_slot:base_slot + 2]
            sv_m = C("vector", lambda e: e.scalar_tensor_tensor(
                out=m2ap, in0=mv2mu, scalar=-1.0, in1=rr[:, base_slot:base_slot + 2],
                op0=Alu.mult, op1=Alu.mult), inc=("sv", 1))
            return sv_m, [mb[:, base_slot + i:base_slot + 1 + i] for i in range(2)]

        S = dict(
            psF_free=(sv_gb, 0), psG_free=(sv_gb, 0), psE_free=(0, 0),
            psPE_free=(0, 0),
            sv_x_consumed=0, sten_hT=0, sten_ffn1=0, sten_ffn2=0, sten_proj=0,
            tr_xd1_prev=NONE_W, tr_xd2_prev=NONE_W,
            sv_xn1_free=0, sv_xn2_free=wfq_done[KC - 1],
            prev_scan=0, sv_carry=0, gscf={0: 0, 1: 0}, sgff={0: 0, 1: 0},
            sg_bby={}, sg_l2b={}, sv_y={}, sv_y2=0, sv_y2_hist={},
            ld_x={}, tr_xdf={}, tr_xd2={}, tr_l2f={}, store_prev=NONE_W,
        )
        S["ld_x"][(0, 0)] = init_ld_x0
        S["ld_x"][(0, 1)] = init_ld_x0

        def emit_A_load(tau):
            r0 = tau * TT
            ld = LD.dma("scalar", lambda e, r0=r0: e.dma_start(
                out=xt[:].rearrange("p (i c) -> p i c", i=NI),
                in_=x_h[r0:r0 + TT, :].rearrange("(i p) c -> p i c", p=128)),
                waits=[("sv", S["sv_x_consumed"])])
            for i in range(NI):
                S["ld_x"][(tau, i)] = ld

        def emit_A(tau):
            svr, r1s, mu1s = ln_stats2(lambda: xt[:], 0, epsln,
                                       [S["ld_x"][(tau, 0)], S["ld_x"][(tau, 1)]])
            for i in range(NI):
                C("vector", lambda e, i=i, r=r1s[i], m=mu1s[i]: e.tensor_scalar(
                    out=xn1[:, i * D:(i + 1) * D], in0=xt[:, i * D:(i + 1) * D],
                    scalar1=m, scalar2=r, op0=Alu.subtract, op1=Alu.mult),
                    waits=[("sv", max(svr, S["sv_xn1_free"]))], drain=(i == NI - 1))
            qa = qsc[:, 0:64]
            qiv = qsc[:, 64:128]
            qi6 = qsc[:, 128:192]
            qs2 = qsc[:, 192:256]
            C("vector", lambda e: e.tensor_reduce(
                out=qa, in_=xn1[:].rearrange("p (b k) -> p b k", k=32),
                axis=AX.X, op=Alu.max, apply_absolute_value=True),
              waits=[S["tr_xd1_prev"]], drain=True)
            C("vector", lambda e: e.reciprocal(out=qiv, in_=qa), drain=True)
            C("vector", lambda e: e.tensor_scalar(
                out=qi6, in0=qiv, scalar1=7.5 * 2.0 ** -6, scalar2=(1e30 * 2.0 ** -6),
                op0=Alu.mult, op1=Alu.min))
            C("vector", lambda e: e.tensor_scalar(
                out=qs2, in0=qa, scalar1=64.0 / 7.5, scalar2=6.4e-29,
                op0=Alu.mult, op1=Alu.max), drain=True)
            sv_q8 = C("vector", lambda e: e.tensor_tensor(
                out=q81[:].rearrange("p (b k) -> p b k", k=32),
                in0=xn1[:].rearrange("p (b k) -> p b k", k=32),
                in1=qi6.unsqueeze(-1).broadcast_to([128, 64, 32]),
                op=Alu.mult), inc=("sv", 1), drain=True)
            sv_xd = 0
            for i in range(NI):
                hi = slice(i * 32, (i + 1) * 32)
                sv_xd = C("vector", lambda e, i=i, hi=hi: e.tensor_tensor(
                    out=xdt1[:, i * D:(i + 1) * D].rearrange("p (b k) -> p b k", k=32),
                    in0=q81[:, i * D:(i + 1) * D].rearrange("p (b k) -> p b k", k=32),
                    in1=qs2[:, hi].unsqueeze(-1).broadcast_to([128, 32, 32]),
                    op=Alu.mult), inc=("sv", 1), drain=(i == NI - 1))
                for hh in range(2):
                    S["tr_xdf"][(tau, i, hh)] = TR.dma("sync", lambda e, i=i, hh=hh: e.dma_start(
                        out=xdf1[:].rearrange("p (k t) -> p k t", k=KC)[
                            :, hh * 4:(hh + 1) * 4, i * 128:(i + 1) * 128],
                        in_=xdt1[:, i * D + hh * 512:i * D + (hh + 1) * 512], transpose=True),
                        waits=[("sv", sv_xd), ("sten", S["sten_ffn1"])])
            S["sv_xn1_free"] = sv_q8
            S["tr_xd1_prev"] = S["tr_xdf"][(tau, NI - 1, 1)]
            C("scalar", lambda e: e.activation(
                out=dmy[:], in_=epsln[:], func=Act.Sigmoid, scale=1.0, bias=0.0))

        def emit_B_pe(tau):
            st_mmB = {}
            for i in range(NI):
                pbk = psF if i == 0 else psE
                freek = "psF_free" if i == 0 else "psE_free"
                mm_wait0 = [S["tr_xdf"][(tau, i, 0)], S["ld_x"][(tau, i)]]
                mm_wait1 = [S["tr_xdf"][(tau, i, 1)]]
                if tau == 0 and i == 0:
                    mm_wait0.append(tr_wdT)
                for half in range(2):
                    C("tensor", lambda e, half=half, pbk=pbk: e.matmul(
                        out=pbk[:, half * 512:(half + 1) * 512], lhsT=ones1[:],
                        rhs=bfh[:, half * 512:(half + 1) * 512], start=True, stop=False),
                      waits=[("sv", max(S[freek][0], sv_bfh)), ("ss", S[freek][1])]
                      if half == 0 else ())
                for half in range(2):
                    for k in range(KC):
                        st_mmB[i] = C("tensor", lambda e, i=i, half=half, k=k, pbk=pbk: e.matmul(
                            out=pbk[:, half * 512:(half + 1) * 512],
                            lhsT=xdf1[:, k * TT + i * 128:k * TT + (i + 1) * 128],
                            rhs=wdT[:, k * 1024 + half * 512:k * 1024 + (half + 1) * 512],
                            start=False, stop=(k == KC - 1)),
                          waits=(mm_wait0 if (half == 0 and k == 0)
                                 else (mm_wait1 if (half == 0 and k == 4) else ())),
                          inc=(("sten", 1) if (half == 1 and k == KC - 1) else None))
            S["st_mmB"] = st_mmB
            S["sten_ffn1"] = st_mmB[NI - 1]

        def emit_B_evac(tau):
            sv_y = 0
            sg_prev_bby = max(S["sg_bby"].get(tau - 1, 0),
                              S["sg_l2b"].get(tau - 1, 0))
            for i in range(NI):
                pbk = psF if i == 0 else psE
                freek = "psF_free" if i == 0 else "psE_free"
                sv_y = C("vector", lambda e, i=i, pbk=pbk: e.tensor_tensor(
                    out=yt[:, i * D:(i + 1) * D], in0=pbk[:],
                    in1=xt[:, i * D:(i + 1) * D], op=Alu.add),
                    waits=[("sten", S["st_mmB"][i]), ("sg", sg_prev_bby)], inc=("sv", 1))
                S[freek] = (sv_y, 0)
                C("gpsimd", lambda e, i=i, par=tau % 2: e.tensor_tensor(
                    out=bby[:, (par * NI + i) * D:(par * NI + i + 1) * D],
                    in0=yt[:, i * D:(i + 1) * D], in1=bb[:], op=Alu.add),
                    waits=[("sv", max(sv_y, sv_gb, S["sv_y2"]))], inc=("sg", 1), drain=True)
            S["sv_x_consumed"] = sv_y
            S["sv_y"][tau] = sv_y
            S["sg_bby"][tau] = ctx.cnt["sg"]

        def emit_C(tau):
            par = tau % 2
            C("vector", lambda e: e.drain())
            svr2, r2s, mu2s = ln_stats2(lambda: yt[:], 2, epsln, [("sv", S["sv_y"][tau])])
            for i in range(NI):
                sg_l2b = C("gpsimd", lambda e, i=i, par=par, m=mu2s[i], r=r2s[i]: e.tensor_scalar(
                    out=l2b[:, (par * 2 + i) * D:(par * 2 + i + 1) * D],
                    in0=yt[:, i * D:(i + 1) * D],
                    scalar1=m, scalar2=r, op0=Alu.subtract, op1=Alu.mult),
                    waits=[("sv", svr2), ("sten", S["sten_hT"])], inc=("sg", 1), drain=True)
                S["tr_l2f"][(tau, i)] = TR.dma("sync", lambda e, i=i, par=par: e.dma_start(
                    out=l2f[:].rearrange("p (k t) -> p k t", k=KC)[:, :, i * 128:(i + 1) * 128],
                    in_=l2b[:, (par * 2 + i) * D:(par * 2 + i + 1) * D], transpose=True),
                    waits=[("sg", sg_l2b), ("sten", S["sten_proj"])])
                S["sg_l2b"][tau] = sg_l2b

        def emit_D(tau):
            st_proj = 0
            scan_prev = dict(S.get("scan_inc", {}))
            S["scan_inc"] = {}
            for jp in range(KC):
                odd = jp % 2
                if odd:
                    pbs = (psF[:, 0:256], psF[:, 512:768])
                    freev = S["psF_free"]
                else:
                    pbs = (psP0[:, 0:256], psP1[:, 0:256])
                    freev = S["psPE_free"]
                st_pj = 0
                for which, jj in enumerate((jp, jp + 8)):
                    pb = pbs[which]
                    w0 = [S["tr_l2f"][(tau, 0)], S["tr_l2f"][(tau, 1)],
                          ("sv", freev[0]), ("ss", freev[1])]
                    if odd:
                        w0.append(("sten", S["sten_ffn1"]))
                    if tau == 0 and jp == 0 and which == 0:
                        w0.append(("sv", sv_Wb))
                    for k in range(KC):
                        st_pj = C("tensor", lambda e, jj=jj, k=k, pb=pb: e.matmul(
                            out=pb,
                            lhsT=Wb[:, k * 2048 + jj * 128:k * 2048 + (jj + 1) * 128],
                            rhs=l2f[:, k * TT:(k + 1) * TT],
                            start=(k == 0), stop=(k == KC - 1)),
                          waits=w0 if k == 0 else (),
                          inc=(("sten", 1) if k == KC - 1 else None))
                st_proj = st_pj
                sl = jp % 2
                ss_sig = C("scalar", lambda e, jp=jp, sl=sl, pb=pbs[0]: e.activation(
                    out=sgf[:, sl * TT:(sl + 1) * TT], in_=pb,
                    func=Act.Sigmoid, scale=1.0, bias=bvec[:, jp:jp + 1]),
                    waits=[("sten", st_pj), ("sv", S["sgff"][sl])], inc=("ss", 1))
                ss_c = C("scalar", lambda e, jp=jp, pb=pbs[1]: e.activation(
                    out=cfull[:, jp * TT:(jp + 1) * TT], in_=pb,
                    func=Act.Sigmoid, scale=-1.0, bias=negb[:, jp + 8:jp + 9]),
                    waits=[("sv", scan_prev.get(jp, 0))], inc=("ss", 1))
                sv_g = C("vector", lambda e, jp=jp, sl=sl, pb=pbs[0]: e.scalar_tensor_tensor(
                    out=gsc[:, sl * TT:(sl + 1) * TT], in0=pb,
                    scalar=b05[:, jp:jp + 1], in1=sgf[:, sl * TT:(sl + 1) * TT],
                    op0=Alu.add, op1=Alu.max),
                    waits=[("ss", ss_sig)], inc=("sv", 1))
                C("vector", lambda e, jp=jp, sl=sl: e.tensor_scalar(
                    out=zsc[:, sl * TT:(sl + 1) * TT],
                    in0=cfull[:, jp * TT:(jp + 1) * TT],
                    scalar1=-1.0, scalar2=1.0, op0=Alu.mult, op1=Alu.add),
                  waits=[("ss", ss_c)], drain=True)
                sv_v = C("vector", lambda e, jp=jp, sl=sl: e.tensor_tensor(
                    out=vfull[:, jp * TT:(jp + 1) * TT],
                    in0=zsc[:, sl * TT:(sl + 1) * TT],
                    in1=gsc[:, sl * TT:(sl + 1) * TT], op=Alu.mult),
                    inc=("sv", 1), drain=True)
                init = 0.0 if tau == 0 else carry[:, jp:jp + 1]
                S["prev_scan"] = C("vector", lambda e, jp=jp, init=init: e.tensor_tensor_scan(
                    out=hf[:, jp * TT:(jp + 1) * TT],
                    data0=cfull[:, jp * TT:(jp + 1) * TT],
                    data1=vfull[:, jp * TT:(jp + 1) * TT], initial=init,
                    op0=Alu.mult, op1=Alu.add),
                    waits=([("sten", S["sten_hT"])] if jp == 0 else ()),
                    inc=("sv", 1), drain=(jp == KC - 1))
                S["scan_inc"][jp] = S["prev_scan"]
                S["sv_v"] = sv_v
                S["sgff"][sl] = sv_g
                if odd:
                    S["psF_free"] = (sv_v, ss_sig)
                else:
                    S["psPE_free"] = (sv_v, ss_sig)
            S["sten_proj"] = st_proj
            C("scalar", lambda e: e.activation(
                out=dmy[:], in_=epsln[:], func=Act.Sqrt, scale=1.0, bias=epsln[:]))
            S["sv_carry"] = C("vector", lambda e: e.tensor_copy(
                out=carry[:],
                in_=hf[:].rearrange("p (k t) -> p k t", k=KC)[:, :, TT - 1:TT].squeeze(-1)),
                inc=("sv", 1), drain=True)

        def emit_E(tau):
            par = tau % 2
            st_hT = 0
            sv_y2 = 0
            for i in range(NI):
                pbk = psE if i == 0 else psG
                freek = "psE_free" if i == 0 else "psG_free"
                w0 = [("sv", S[freek][0]), ("ss", S[freek][1])]
                for half in range(2):
                    C("tensor", lambda e, i=i, half=half, par=par, pbk=pbk: e.matmul(
                        out=pbk[:, half * 512:(half + 1) * 512], lhsT=identb[:],
                        rhs=l2b[:, (par * 2 + i) * D + half * 512:
                                (par * 2 + i) * D + (half + 1) * 512],
                        start=True, stop=False, skip_group_check=True),
                      waits=w0 if half == 0 else ())
                for j in range(KC):
                    st_hT = C("tensor", lambda e, i=i, j=j, pbk=pbk: e.matmul(
                        out=pbk[:, j * 128:(j + 1) * 128],
                        lhsT=hf[:, j * TT + i * 128:j * TT + (i + 1) * 128],
                        rhs=ident[:], is_transpose=True,
                        start=False, stop=(j == KC - 1), skip_group_check=True),
                      waits=([("sv", S["scan_inc"][j])] if (i == 0 or j == 0) else ()),
                      inc=(("sten", 1) if j == KC - 1 else None)) or st_hT
                s6 = bns[:, (4 + i) * 12:(4 + i) * 12 + 12]
                C("vector", lambda e, s6=s6, pbk=pbk: e.bn_stats(out=s6[:, 0:6], in_=pbk[:, 0:512]),
                  waits=[("sten", st_hT)])
                C("vector", lambda e, s6=s6, pbk=pbk: e.bn_stats(out=s6[:, 6:12], in_=pbk[:, 512:1024]),
                  drain=True)
                mvs = mv[:, (4 + i) * 2:(4 + i) * 2 + 2]
                sv_mv = C("vector", lambda e, s6=s6, mvs=mvs: e.bn_aggr(out=mvs, in_=s6),
                          inc=("sv", 1))
                sd = sden[:, 4 + i:5 + i]
                ss_sq = C("scalar", lambda e, sd=sd, mvs=mvs: e.activation(
                    out=sd, in_=mvs[:, 1:2], func=Act.Sqrt, bias=epsat[:], scale=1.0),
                    waits=[("sv", sv_mv)], inc=("ss", 1))
                r3 = rr[:, 4 + i:5 + i]
                C("vector", lambda e, r3=r3, sd=sd: e.reciprocal(out=r3, in_=sd),
                  waits=[("ss", ss_sq)], drain=True)
                sv_n = C("vector", lambda e, mvs=mvs, r3=r3, pbk=pbk: e.tensor_scalar(
                    out=nt[:, 0:D], in0=pbk[:],
                    scalar1=mvs[:, 0:1], scalar2=r3, op0=Alu.subtract, op1=Alu.mult),
                    inc=("sv", 1), drain=True)
                S[freek] = (sv_n, 0)
                C("vector", lambda e: e.tensor_tensor(
                    out=t1t, in0=nt[:, 0:D], in1=gb[:], op=Alu.mult),
                  waits=[("sg", S["sg_bby"][tau])], drain=True)
                sv_y2 = C("vector", lambda e, i=i, par=par: e.tensor_tensor(
                    out=y2t[:, i * D:(i + 1) * D], in0=t1t,
                    in1=bby[:, (par * NI + i) * D:(par * NI + i + 1) * D], op=Alu.add),
                    inc=("sv", 1))
            S["sten_hT"] = st_hT
            S["sv_y2"] = sv_y2
            S["sv_y2_hist"][tau] = sv_y2

        def emit_F(tau):
            C("vector", lambda e: e.drain())
            svr4, r4s, mu4s = ln_stats2(lambda: y2t[:], 2, epsln, [("sv", S["sv_y2_hist"][tau])])
            for i in range(NI):
                C("vector", lambda e, i=i, r=r4s[i], m=mu4s[i]: e.tensor_scalar(
                    out=xn2[:, i * D:(i + 1) * D], in0=y2t[:, i * D:(i + 1) * D],
                    scalar1=m, scalar2=r, op0=Alu.subtract, op1=Alu.mult),
                    waits=[("sv", max(svr4, S["sv_xn2_free"]))], drain=(i == NI - 1))
            qa2 = qsc[:, 256 + 0:256 + 64]
            qiv2 = qsc[:, 256 + 64:256 + 128]
            qi62 = qsc[:, 256 + 128:256 + 192]
            qs22 = qsc[:, 256 + 192:256 + 256]
            C("vector", lambda e: e.tensor_reduce(
                out=qa2, in_=xn2[:].rearrange("p (b k) -> p b k", k=32),
                axis=AX.X, op=Alu.max, apply_absolute_value=True),
              waits=[S["tr_xd2_prev"]], drain=True)
            C("vector", lambda e: e.reciprocal(out=qiv2, in_=qa2), drain=True)
            C("vector", lambda e: e.tensor_scalar(
                out=qi62, in0=qiv2, scalar1=7.5 * 2.0 ** -6, scalar2=(1e30 * 2.0 ** -6),
                op0=Alu.mult, op1=Alu.min))
            C("vector", lambda e: e.tensor_scalar(
                out=qs22, in0=qa2, scalar1=64.0 / 7.5, scalar2=6.4e-29,
                op0=Alu.mult, op1=Alu.max), drain=True)
            sv_q82 = C("vector", lambda e: e.tensor_tensor(
                out=q82[:].rearrange("p (b k) -> p b k", k=32),
                in0=xn2[:].rearrange("p (b k) -> p b k", k=32),
                in1=qi62.unsqueeze(-1).broadcast_to([128, 64, 32]),
                op=Alu.mult), inc=("sv", 1), drain=True)
            sv_xd2 = 0
            for i in range(NI):
                hi = slice(i * 32, (i + 1) * 32)
                sv_xd2 = C("vector", lambda e, i=i, hi=hi: e.tensor_tensor(
                    out=xdt2[:, i * D:(i + 1) * D].rearrange("p (b k) -> p b k", k=32),
                    in0=q82[:, i * D:(i + 1) * D].rearrange("p (b k) -> p b k", k=32),
                    in1=qs22[:, hi].unsqueeze(-1).broadcast_to([128, 32, 32]),
                    op=Alu.mult), inc=("sv", 1), drain=(i == NI - 1))
                for hh in range(2):
                    S["tr_xd2"][(tau, i, hh)] = TR.dma("sync", lambda e, i=i, hh=hh: e.dma_start(
                        out=xdf2[:].rearrange("p (k t) -> p k t", k=KC)[
                            :, hh * 4:(hh + 1) * 4, i * 128:(i + 1) * 128],
                        in_=xdt2[:, i * D + hh * 512:i * D + (hh + 1) * 512], transpose=True),
                        waits=[("sv", sv_xd2), ("sten", S["sten_ffn2"])])
            S["sv_xn2_free"] = sv_q82
            S["tr_xd2_prev"] = S["tr_xd2"][(tau, NI - 1, 1)]

        def emit_G_pe(tau):
            st_mmG = {}
            for i in range(NI):
                halves = ([psP0[:], psP1[:]] if i == 0
                          else [psG[:, 0:512], psG[:, 512:1024]])
                freek = "psPE_free" if i == 0 else "psG_free"
                mm_wait0 = [S["tr_xd2"][(tau, i, 0)]]
                mm_wait1 = [S["tr_xd2"][(tau, i, 1)]]
                for half in range(2):
                    C("tensor", lambda e, half=half, halves=halves: e.matmul(
                        out=halves[half], lhsT=ones1[:],
                        rhs=bfh[:, half * 512:(half + 1) * 512], start=True, stop=False),
                      waits=[("sv", max(S[freek][0], sv_bfh)), ("ss", S[freek][1])]
                      if half == 0 else ())
                for half in range(2):
                    for k in range(KC):
                        st_mmG[i] = C("tensor", lambda e, i=i, half=half, k=k, halves=halves: e.matmul(
                            out=halves[half],
                            lhsT=xdf2[:, k * TT + i * 128:k * TT + (i + 1) * 128],
                            rhs=wdT[:, k * 1024 + half * 512:k * 1024 + (half + 1) * 512],
                            start=False, stop=(k == KC - 1)),
                          waits=(mm_wait0 if (half == 0 and k == 0)
                                 else (mm_wait1 if (half == 0 and k == 4) else ())),
                          inc=(("sten", 1) if (half == 1 and k == KC - 1) else None))
            S["st_mmG"] = st_mmG
            S["sten_ffn2"] = st_mmG[NI - 1]

        def emit_G_evac(tau, which):
            for i in which:
                halves = ([psP0[:], psP1[:]] if i == 0
                          else [psG[:, 0:512], psG[:, 512:1024]])
                freek = "psPE_free" if i == 0 else "psG_free"
                sv_out = 0
                for half in range(2):
                    sv_out = C("vector", lambda e, i=i, half=half, halves=halves: e.tensor_tensor(
                        out=outt[:, half * 512:(half + 1) * 512], in0=halves[half],
                        in1=y2t[:, i * D + half * 512:i * D + (half + 1) * 512], op=Alu.add),
                        waits=([("sten", S["st_mmG"][i]), S["store_prev"]] if half == 0 else ()),
                        inc=("sv", 1))
                S[freek] = (sv_out, 0)
                r0 = tau * TT + i * 128
                S["store_prev"] = STP.dma("scalar", lambda e, r0=r0: e.dma_start(
                    out=out_h[r0:r0 + 128, :], in_=outt[:, 0:D]),
                    waits=[("sv", sv_out)])

        emit_A(0)
        emit_B_pe(0)
        cast_done = {}
        wstages = [Wst[:], xn1[:], yt[:], l2b[:].bitcast(F32)]
        for k in range(KC):
            stg = wstages[k % 4]
            ldWk = LD.dma("scalar", lambda e, k=k, stg=stg: e.dma_start(
                out=stg, in_=W_h[k * 128:(k + 1) * 128, :]),
                waits=[("sv", max(cast_done.get(k - 4, 0),
                                  S["sv_xn1_free"] if k % 4 == 1 else 0))])
            cast_done[k] = C("vector", lambda e, k=k, stg=stg: e.tensor_copy(
                out=Wb[:, k * 2048:(k + 1) * 2048], in_=stg),
                waits=[ldWk], inc=("sv", 1), drain=True)
        sv_Wb = cast_done[KC - 1]
        S["sv_xn1_free"] = cast_done[5]
        emit_B_evac(0)
        emit_C(0)
        emit_A_load(1)
        emit_A(1)
        for tau in range(NTAU):
            emit_D(tau)
            emit_E(tau)
            emit_F(tau)
            if tau + 1 < NTAU:
                emit_B_pe(tau + 1)
            emit_G_pe(tau)
            if tau + 1 < NTAU:
                emit_B_evac(tau + 1)
                emit_A_load(tau + 2) if tau + 2 < NTAU else None
                emit_C(tau + 1)
            emit_G_evac(tau, (0,))
            if tau + 2 < NTAU:
                emit_A(tau + 2)
            emit_G_evac(tau, (1,))

        C("gpsimd", lambda e: None, waits=[S["store_prev"]])

        @block.sync
        def _(eng):
            for f in ctx.progs["sync"]:
                f(eng, sems)

        @block.vector
        def _(eng):
            for f in ctx.progs["vector"]:
                f(eng, sems)

        @block.scalar
        def _(eng):
            for f in ctx.progs["scalar"]:
                f(eng, sems)

        @block.tensor
        def _(eng):
            for f in ctx.progs["tensor"]:
                f(eng, sems)

        @block.gpsimd
        def _(eng):
            for f in ctx.progs["gpsimd"]:
                f(eng, sems)

    return nc


_NC_CACHE = {}


def kernel(L0_attn_x, L0_attn_W, L0_attn_b, L0_attn_gamma, L0_attn_beta,
           L0_ffn_weight, L0_ffn_bias):
    x = np.ascontiguousarray(np.asarray(L0_attn_x, dtype=np.float32))
    W = np.ascontiguousarray(np.asarray(L0_attn_W, dtype=np.float32))
    b = np.ascontiguousarray(np.asarray(L0_attn_b, dtype=np.float32))
    gamma = np.ascontiguousarray(np.asarray(L0_attn_gamma, dtype=np.float32))
    beta = np.ascontiguousarray(np.asarray(L0_attn_beta, dtype=np.float32))
    wf = np.ascontiguousarray(np.asarray(L0_ffn_weight, dtype=np.float32))
    bf_ = np.ascontiguousarray(np.asarray(L0_ffn_bias, dtype=np.float32))
    iden = np.eye(128, dtype=np.float32)

    if "nc" not in _NC_CACHE:
        _NC_CACHE["nc"] = build_nc()
    nc = _NC_CACHE["nc"]
    in_maps = [{"x": x[i], "W": W, "b": b, "gamma": gamma, "beta": beta,
                "wf": wf, "bf": bf_, "iden": iden} for i in range(8)]
    res = run_bass_kernel_spmd(nc, in_maps, core_ids=list(range(8)))
    return np.stack([np.asarray(r["out"]) for r in res.results], axis=0)


# version 37
# speedup vs baseline: 1.2845x; 1.2845x over previous
import contextlib
import numpy as np
import concourse.bass as bass
import concourse.mybir as mybir
from concourse.bass_utils import run_bass_kernel_spmd

F32 = mybir.dt.float32
BF16 = mybir.dt.bfloat16
FP8 = mybir.dt.float8e4
Alu = mybir.AluOpType
Act = mybir.ActivationFunctionType
AX = mybir.AxisListType

T = 2048
D = 1024
TT = 256
NTAU = T // TT
NI = TT // 128
KC = 8
E_LN = 1e-5
E_ATT = 1e-6
N_LD = 8
N_TR = 10
N_ST = 4


class Ctx:
    def __init__(self):
        self.cnt = {}
        self.progs = {"sync": [], "vector": [], "scalar": [], "tensor": [], "gpsimd": []}

    def op(self, engine, fn, waits=(), inc=None, drain=False):
        after = None
        if inc is not None:
            sem, amt = inc
            self.cnt[sem] = self.cnt.get(sem, 0) + amt
            after = self.cnt[sem]
        waits = tuple((s, v) for (s, v) in waits if s is not None and v > 0)

        def emit(eng, sems, fn=fn, waits=waits, inc=inc, drain=drain):
            for s, v in waits:
                eng.wait_ge(sems[s], v)
            instr = fn(eng)
            if inc is not None:
                if instr is None:
                    instr = eng.nop()
                instr.then_inc(sems[inc[0]], inc[1])
            if drain:
                eng.drain()
        self.progs[engine].append(emit)
        return after


class DmaPool:
    def __init__(self, ctx, name, nslots):
        self.ctx, self.name, self.n = ctx, name, nslots
        self.uses = [0] * nslots
        self.rr = 0

    def dma(self, queue, fn, waits=()):
        s = self.rr
        self.rr = (self.rr + 1) % self.n
        semname = "%s%d" % (self.name, s)
        w = list(waits)
        if self.uses[s] > 0:
            w.append((semname, 16 * self.uses[s]))
        self.uses[s] += 1
        val = self.ctx.op(queue, fn, waits=w, inc=(semname, 16))
        return (semname, val)


NONE_W = ("sv", 0)


def build_nc():
    nc = bass.Bass("TRN2", target_bir_lowering=False)
    x_h = nc.dram_tensor("x", [T, D], F32, kind="ExternalInput").ap()
    W_h = nc.dram_tensor("W", [D, 2 * D], F32, kind="ExternalInput").ap()
    b_h = nc.dram_tensor("b", [2 * D], F32, kind="ExternalInput").ap()
    gamma_h = nc.dram_tensor("gamma", [D], F32, kind="ExternalInput").ap()
    beta_h = nc.dram_tensor("beta", [D], F32, kind="ExternalInput").ap()
    wf_h = nc.dram_tensor("wf", [D, D], F32, kind="ExternalInput").ap()
    bf_h = nc.dram_tensor("bf", [D], F32, kind="ExternalInput").ap()
    iden_h = nc.dram_tensor("iden", [128, 128], F32, kind="ExternalInput").ap()
    out_h = nc.dram_tensor("out", [T, D], F32, kind="ExternalOutput").ap()

    ctx = Ctx()
    C = ctx.op
    LD = DmaPool(ctx, "ld", N_LD)
    TR = DmaPool(ctx, "tr", N_TR)
    STP = DmaPool(ctx, "st", N_ST)

    es = contextlib.ExitStack()
    with es:
        Wb = es.enter_context(nc.sbuf_tensor("Wb", [128, KC * 2048], BF16))
        wdT = es.enter_context(nc.sbuf_tensor("wdT", [128, KC * 1024], BF16))
        gb = es.enter_context(nc.sbuf_tensor("gb", [128, D], F32))
        bb = es.enter_context(nc.sbuf_tensor("bb", [128, D], F32))
        ident = es.enter_context(nc.sbuf_tensor("ident", [128, 128], F32))
        identb = es.enter_context(nc.sbuf_tensor("identb", [128, 128], BF16))
        ones1 = es.enter_context(nc.sbuf_tensor("ones1", [1, 128], F32))
        grow = es.enter_context(nc.sbuf_tensor("grow", [1, 2 * D], F32))
        bfrow = es.enter_context(nc.sbuf_tensor("bfrow", [1, D], F32))
        bfh = es.enter_context(nc.sbuf_tensor("bfh", [1, D], F32))
        bvec = es.enter_context(nc.sbuf_tensor("bvec", [128, 16], F32))
        negb = es.enter_context(nc.sbuf_tensor("negb", [128, 16], F32))
        b05 = es.enter_context(nc.sbuf_tensor("b05", [128, 16], F32))
        epsln = es.enter_context(nc.sbuf_tensor("epsln", [128, 1], F32))
        epsat = es.enter_context(nc.sbuf_tensor("epsat", [128, 1], F32))
        Wst = es.enter_context(nc.sbuf_tensor("Wst", [128, 2048], F32))
        wfst = es.enter_context(nc.sbuf_tensor("wfst", [128, 1024], F32))
        wq8 = es.enter_context(nc.sbuf_tensor("wq8", [128, 1024], FP8))
        wdtok = es.enter_context(nc.sbuf_tensor("wdtok", [128, 1024], BF16))
        wsc = es.enter_context(nc.sbuf_tensor("wsc", [128, 128], F32))
        xt = es.enter_context(nc.sbuf_tensor("xt", [128, NI * D], F32))
        xn1 = es.enter_context(nc.sbuf_tensor("xn1", [128, 2 * D], F32))
        xn2 = es.enter_context(nc.sbuf_tensor("xn2", [128, 2 * D], F32))
        q81 = es.enter_context(nc.sbuf_tensor("q81", [128, 2 * D], FP8))
        q82 = q81
        xdt1 = es.enter_context(nc.sbuf_tensor("xdt1", [128, 2 * D], BF16))
        xdt2 = es.enter_context(nc.sbuf_tensor("xdt2", [128, 2 * D], BF16))
        xdf1 = es.enter_context(nc.sbuf_tensor("xdf1", [128, KC * TT], BF16))
        xdf2 = es.enter_context(nc.sbuf_tensor("xdf2", [128, KC * TT], BF16))
        yt = es.enter_context(nc.sbuf_tensor("yt", [128, NI * D], F32))
        y2t = Wst[:, 0:2 * D]
        t1t = wfst[:, 0:D]
        bby = es.enter_context(nc.sbuf_tensor("bby", [128, 2 * NI * D], F32))
        l2b = es.enter_context(nc.sbuf_tensor("l2b", [128, 2 * 2 * D], BF16))
        l2f = es.enter_context(nc.sbuf_tensor("l2f", [128, KC * TT], BF16))
        cfull = es.enter_context(nc.sbuf_tensor("cfull", [128, KC * TT], F32))
        vfull = es.enter_context(nc.sbuf_tensor("vfull", [128, KC * TT], F32))
        gsc = es.enter_context(nc.sbuf_tensor("gsc", [128, 2 * TT], F32))
        sgf = es.enter_context(nc.sbuf_tensor("sgf", [128, 2 * TT], F32))
        zsc = wdtok[:].bitcast(F32)
        hf = es.enter_context(nc.sbuf_tensor("hf", [128, KC * TT], F32))
        carry = es.enter_context(nc.sbuf_tensor("carry", [128, KC], F32))
        nt = es.enter_context(nc.sbuf_tensor("nt", [128, D], F32))
        outt = es.enter_context(nc.sbuf_tensor("outt", [128, D], F32))
        bns = es.enter_context(nc.sbuf_tensor("bns", [128, 72], F32))
        mv = es.enter_context(nc.sbuf_tensor("mv", [128, 12], F32))
        sden = es.enter_context(nc.sbuf_tensor("sden", [128, 6], F32))
        rr = es.enter_context(nc.sbuf_tensor("rr", [128, 6], F32))
        mb = es.enter_context(nc.sbuf_tensor("mb", [128, 6], F32))
        qsc = es.enter_context(nc.sbuf_tensor("qsc", [128, 512], F32))
        dmy = es.enter_context(nc.sbuf_tensor("dmy", [128, 1], F32))
        psF = es.enter_context(nc.psum_tensor("psF", [128, 1024], F32))
        psG = es.enter_context(nc.psum_tensor("psG", [128, 1024], F32))
        psE = es.enter_context(nc.psum_tensor("psE", [128, 1024], F32))
        psP0 = es.enter_context(nc.psum_tensor("psP0", [128, 512], F32))
        psP1 = es.enter_context(nc.psum_tensor("psP1", [128, 512], F32))
        sems = {}
        for nm in (["sv", "ss", "sten", "sg"]
                   + ["ld%d" % i for i in range(N_LD)]
                   + ["tr%d" % i for i in range(N_TR)]
                   + ["st%d" % i for i in range(N_ST)]):
            sems[nm] = es.enter_context(nc.semaphore(nm))
        block = es.enter_context(nc.Block())
        psP = [psP0, psP1]

        ld_iden = LD.dma("sync", lambda e: e.dma_start(out=ident[:], in_=iden_h))
        ld_grow = LD.dma("sync", lambda e: e.dma_start(out=grow[:, 0:D], in_=gamma_h.unsqueeze(0)))
        ld_brow = LD.dma("sync", lambda e: e.dma_start(out=grow[:, D:2 * D], in_=beta_h.unsqueeze(0)))
        ld_bfrow = LD.dma("sync", lambda e: e.dma_start(out=bfrow[:], in_=bf_h.unsqueeze(0)))
        ld_bvec = LD.dma("sync", lambda e: e.dma_start(
            out=bvec[:], in_=b_h.rearrange("(j p) -> p j", p=128),
            allow_slow_non_contiguous=True))

        C("vector", lambda e: e.memset(epsln[:], E_LN))
        C("vector", lambda e: e.memset(epsat[:], E_ATT))
        C("vector", lambda e: e.memset(ones1[:], 1.0))
        sv_ones = C("vector", lambda e: e.tensor_copy(out=identb[:], in_=ident[:]),
                    waits=[ld_iden], inc=("sv", 1))

        prev_q = 0
        prev_wtr = NONE_W
        wfq_done = {}
        init_ld_x0 = None
        for k in range(KC):
            if k == 2:
                init_ld_x0 = LD.dma("scalar", lambda e: e.dma_start(
                    out=xt[:].rearrange("p (i c) -> p i c", i=NI),
                    in_=x_h[0:TT, :].rearrange("(i p) c -> p i c", p=128)))
            wstg = wfst[:, 0:D] if k % 2 == 0 else xn2[:, 0:D]
            ldwf = LD.dma("scalar", lambda e, k=k, wstg=wstg: e.dma_start(
                out=wstg, in_=wf_h[k * 128:(k + 1) * 128, :]),
                waits=[("sv", wfq_done.get(k - 2, 0))])
            wam, wscl, wi6, ws2 = (wsc[:, 0:32], wsc[:, 32:64], wsc[:, 64:96], wsc[:, 96:128])
            C("vector", lambda e, wstg=wstg: e.tensor_reduce(
                out=wam, in_=wstg.rearrange("p (b k) -> p b k", k=32),
                axis=AX.X, op=Alu.max, apply_absolute_value=True),
              waits=[ldwf], drain=True)
            C("vector", lambda e: e.tensor_scalar(
                out=wscl, in0=wam, scalar1=1.0 / 7.5, scalar2=1e-30,
                op0=Alu.mult, op1=Alu.max), drain=True)
            C("vector", lambda e: e.reciprocal(out=wi6, in_=wscl), drain=True)
            C("vector", lambda e: e.tensor_scalar(
                out=wi6, in0=wi6, scalar1=2.0 ** -6, scalar2=None, op0=Alu.mult), drain=True)
            C("vector", lambda e: e.tensor_scalar(
                out=ws2, in0=wscl, scalar1=32.0, scalar2=None, op0=Alu.mult), drain=True)
            wfq_done[k] = C("vector", lambda e, wstg=wstg: e.tensor_tensor(
                out=wq8[:].rearrange("p (b k) -> p b k", k=32),
                in0=wstg.rearrange("p (b k) -> p b k", k=32),
                in1=wi6.unsqueeze(-1).broadcast_to([128, 32, 32]), op=Alu.mult),
                inc=("sv", 1), drain=True)
            prev_q = C("vector", lambda e: e.tensor_tensor(
                out=wdtok[:].rearrange("p (b k) -> p b k", k=32),
                in0=wq8[:].rearrange("p (b k) -> p b k", k=32),
                in1=ws2.unsqueeze(-1).broadcast_to([128, 32, 32]), op=Alu.mult),
                waits=[prev_wtr], inc=("sv", 1), drain=True)
            prev_wtr = TR.dma("sync", lambda e, k=k: e.dma_start(
                out=wdT[:].rearrange("p (m c l) -> p m c l", m=KC, c=KC)[:, :, k, :],
                in_=wdtok[:], transpose=True),
                waits=[("sv", prev_q)])
        tr_wdT = prev_wtr

        C("vector", lambda e: e.tensor_scalar(out=negb[:], in0=bvec[:], scalar1=-1.0,
                                              scalar2=None, op0=Alu.mult),
          waits=[ld_bvec])
        sv_bvecs = C("vector", lambda e: e.tensor_scalar(out=b05[:], in0=bvec[:], scalar1=0.5,
                                                         scalar2=None, op0=Alu.add),
                     inc=("sv", 1))
        sv_bfh = C("vector", lambda e: e.tensor_scalar(out=bfh[:], in0=bfrow[:], scalar1=0.5,
                                                       scalar2=None, op0=Alu.mult),
                   waits=[ld_bfrow], inc=("sv", 1), drain=True)

        st_gbmm = 0
        for half in range(2):
            C("tensor", lambda e, h=half: e.matmul(
                out=psF[:, h * 512:(h + 1) * 512], lhsT=ones1[:],
                rhs=grow[:, h * 512:(h + 1) * 512], start=True, stop=True),
              waits=([ld_grow, ld_iden, ld_brow, ("sv", sv_ones)] if half == 0 else ()))
            st_gbmm = C("tensor", lambda e, h=half: e.matmul(
                out=psG[:, h * 512:(h + 1) * 512], lhsT=ones1[:],
                rhs=grow[:, D + h * 512:D + (h + 1) * 512], start=True, stop=True),
                inc=(("sten", 1) if half == 1 else None)) or st_gbmm
        C("vector", lambda e: e.tensor_copy(out=gb[:], in_=psF[:]),
          waits=[("sten", st_gbmm)])
        sv_gb = C("vector", lambda e: e.tensor_copy(out=bb[:], in_=psG[:]),
                  inc=("sv", 1), drain=True)

        def ln_stats2(src2_getter, base_slot, eps_tile, wait_list):
            s6a = bns[:, base_slot * 12:base_slot * 12 + 12]
            s6b = bns[:, (base_slot + 1) * 12:(base_slot + 1) * 12 + 12]
            C("vector", lambda e: e.bn_stats(out=s6a[:, 0:6], in_=src2_getter()[:, 0:512]),
              waits=wait_list)
            C("vector", lambda e: e.bn_stats(out=s6a[:, 6:12], in_=src2_getter()[:, 512:1024]))
            C("vector", lambda e: e.bn_stats(out=s6b[:, 0:6], in_=src2_getter()[:, 1024:1536]))
            C("vector", lambda e: e.bn_stats(out=s6b[:, 6:12], in_=src2_getter()[:, 1536:2048]),
              drain=True)
            mv2 = mv[:, base_slot * 2:base_slot * 2 + 4]
            C("vector", lambda e: e.bn_aggr(out=mv2[:, 0:2], in_=s6a))
            sv_mv = C("vector", lambda e: e.bn_aggr(out=mv2[:, 2:4], in_=s6b),
                      inc=("sv", 1))
            sd = sden[:, base_slot:base_slot + 2]
            varap = mv2.rearrange("p (s c) -> p s c", c=2)[:, :, 1:2].squeeze(-1)
            ss_sq = C("scalar", lambda e: e.activation(
                out=sd, in_=varap, func=Act.Sqrt, bias=eps_tile[:], scale=1.0),
                waits=[("sv", sv_mv)], inc=("ss", 1))
            r2ap = rr[:, base_slot:base_slot + 2]
            sv_r = C("vector", lambda e: e.reciprocal(out=r2ap, in_=sd),
                     waits=[("ss", ss_sq)], inc=("sv", 1), drain=True)
            rs = [rr[:, base_slot + i:base_slot + 1 + i] for i in range(2)]
            mus = [mv[:, base_slot * 2 + 2 * i:base_slot * 2 + 2 * i + 1] for i in range(2)]
            return sv_r, rs, mus

        def mkmb(base_slot, sv_r):
            mv2mu = mv[:, base_slot * 2:base_slot * 2 + 4].rearrange(
                "p (s c) -> p s c", c=2)[:, :, 0:1].squeeze(-1)
            m2ap = mb[:, base_slot:base_slot + 2]
            sv_m = C("vector", lambda e: e.scalar_tensor_tensor(
                out=m2ap, in0=mv2mu, scalar=-1.0, in1=rr[:, base_slot:base_slot + 2],
                op0=Alu.mult, op1=Alu.mult), inc=("sv", 1))
            return sv_m, [mb[:, base_slot + i:base_slot + 1 + i] for i in range(2)]

        S = dict(
            psF_free=(sv_gb, 0), psG_free=(sv_gb, 0), psE_free=(0, 0),
            psPE_free=(0, 0),
            sv_x_consumed=0, sten_hT=0, sten_ffn1=0, sten_ffn2=0, sten_proj=0,
            tr_xd1_prev=NONE_W, tr_xd2_prev=NONE_W,
            sv_xn1_free=0, sv_xn2_free=wfq_done[KC - 1],
            prev_scan=0, sv_carry=0, gscf={0: 0, 1: 0}, sgff={0: 0, 1: 0},
            sg_bby={}, sv_y={}, sv_y2=0, sv_y2_hist={},
            ld_x={}, tr_xdf={}, tr_xd2={}, tr_l2f={}, store_prev=NONE_W,
        )
        S["ld_x"][(0, 0)] = init_ld_x0
        S["ld_x"][(0, 1)] = init_ld_x0

        def emit_A_load(tau):
            r0 = tau * TT
            ld = LD.dma("scalar", lambda e, r0=r0: e.dma_start(
                out=xt[:].rearrange("p (i c) -> p i c", i=NI),
                in_=x_h[r0:r0 + TT, :].rearrange("(i p) c -> p i c", p=128)),
                waits=[("sv", S["sv_x_consumed"])])
            for i in range(NI):
                S["ld_x"][(tau, i)] = ld

        def emit_A(tau):
            svr, r1s, mu1s = ln_stats2(lambda: xt[:], 0, epsln,
                                       [S["ld_x"][(tau, 0)], S["ld_x"][(tau, 1)]])
            for i in range(NI):
                C("vector", lambda e, i=i, r=r1s[i], m=mu1s[i]: e.tensor_scalar(
                    out=xn1[:, i * D:(i + 1) * D], in0=xt[:, i * D:(i + 1) * D],
                    scalar1=m, scalar2=r, op0=Alu.subtract, op1=Alu.mult),
                    waits=[("sv", max(svr, S["sv_xn1_free"]))], drain=(i == NI - 1))
            qa = qsc[:, 0:64]
            qiv = qsc[:, 64:128]
            qi6 = qsc[:, 128:192]
            qs2 = qsc[:, 192:256]
            C("vector", lambda e: e.tensor_reduce(
                out=qa, in_=xn1[:].rearrange("p (b k) -> p b k", k=32),
                axis=AX.X, op=Alu.max, apply_absolute_value=True),
              waits=[S["tr_xd1_prev"]], drain=True)
            C("vector", lambda e: e.reciprocal(out=qiv, in_=qa), drain=True)
            C("vector", lambda e: e.tensor_scalar(
                out=qi6, in0=qiv, scalar1=7.5 * 2.0 ** -6, scalar2=(1e30 * 2.0 ** -6),
                op0=Alu.mult, op1=Alu.min))
            C("vector", lambda e: e.tensor_scalar(
                out=qs2, in0=qa, scalar1=64.0 / 7.5, scalar2=6.4e-29,
                op0=Alu.mult, op1=Alu.max), drain=True)
            sv_q8 = C("vector", lambda e: e.tensor_tensor(
                out=q81[:].rearrange("p (b k) -> p b k", k=32),
                in0=xn1[:].rearrange("p (b k) -> p b k", k=32),
                in1=qi6.unsqueeze(-1).broadcast_to([128, 64, 32]),
                op=Alu.mult), inc=("sv", 1), drain=True)
            sv_xd = 0
            for i in range(NI):
                hi = slice(i * 32, (i + 1) * 32)
                sv_xd = C("vector", lambda e, i=i, hi=hi: e.tensor_tensor(
                    out=xdt1[:, i * D:(i + 1) * D].rearrange("p (b k) -> p b k", k=32),
                    in0=q81[:, i * D:(i + 1) * D].rearrange("p (b k) -> p b k", k=32),
                    in1=qs2[:, hi].unsqueeze(-1).broadcast_to([128, 32, 32]),
                    op=Alu.mult), inc=("sv", 1), drain=(i == NI - 1))
                for hh in range(2):
                    S["tr_xdf"][(tau, i, hh)] = TR.dma("sync", lambda e, i=i, hh=hh: e.dma_start(
                        out=xdf1[:].rearrange("p (k t) -> p k t", k=KC)[
                            :, hh * 4:(hh + 1) * 4, i * 128:(i + 1) * 128],
                        in_=xdt1[:, i * D + hh * 512:i * D + (hh + 1) * 512], transpose=True),
                        waits=[("sv", sv_xd), ("sten", S["sten_ffn1"])])
            S["sv_xn1_free"] = sv_q8
            S["tr_xd1_prev"] = S["tr_xdf"][(tau, NI - 1, 1)]
            C("scalar", lambda e: e.activation(
                out=dmy[:], in_=epsln[:], func=Act.Sigmoid, scale=1.0, bias=0.0))

        def emit_B_pe(tau):
            st_mmB = {}
            for i in range(NI):
                pbk = psF if i == 0 else psE
                freek = "psF_free" if i == 0 else "psE_free"
                mm_wait0 = [S["tr_xdf"][(tau, i, 0)], S["ld_x"][(tau, i)]]
                mm_wait1 = [S["tr_xdf"][(tau, i, 1)]]
                if tau == 0 and i == 0:
                    mm_wait0.append(tr_wdT)
                for half in range(2):
                    C("tensor", lambda e, half=half, pbk=pbk: e.matmul(
                        out=pbk[:, half * 512:(half + 1) * 512], lhsT=ones1[:],
                        rhs=bfh[:, half * 512:(half + 1) * 512], start=True, stop=False),
                      waits=[("sv", max(S[freek][0], sv_bfh)), ("ss", S[freek][1])]
                      if half == 0 else ())
                for half in range(2):
                    for k in range(KC):
                        st_mmB[i] = C("tensor", lambda e, i=i, half=half, k=k, pbk=pbk: e.matmul(
                            out=pbk[:, half * 512:(half + 1) * 512],
                            lhsT=xdf1[:, k * TT + i * 128:k * TT + (i + 1) * 128],
                            rhs=wdT[:, k * 1024 + half * 512:k * 1024 + (half + 1) * 512],
                            start=False, stop=(k == KC - 1)),
                          waits=(mm_wait0 if (half == 0 and k == 0)
                                 else (mm_wait1 if (half == 0 and k == 4) else ())),
                          inc=(("sten", 1) if (half == 1 and k == KC - 1) else None))
            S["st_mmB"] = st_mmB
            S["sten_ffn1"] = st_mmB[NI - 1]

        def emit_B_evac(tau):
            sv_y = 0
            sg_prev_bby = S["sg_bby"].get(tau - 1, 0)
            for i in range(NI):
                pbk = psF if i == 0 else psE
                freek = "psF_free" if i == 0 else "psE_free"
                sv_y = C("vector", lambda e, i=i, pbk=pbk: e.tensor_tensor(
                    out=yt[:, i * D:(i + 1) * D], in0=pbk[:],
                    in1=xt[:, i * D:(i + 1) * D], op=Alu.add),
                    waits=[("sten", S["st_mmB"][i]), ("sg", sg_prev_bby)], inc=("sv", 1))
                S[freek] = (sv_y, 0)
                C("gpsimd", lambda e, i=i, par=tau % 2: e.tensor_tensor(
                    out=bby[:, (par * NI + i) * D:(par * NI + i + 1) * D],
                    in0=yt[:, i * D:(i + 1) * D], in1=bb[:], op=Alu.add),
                    waits=[("sv", max(sv_y, sv_gb, S["sv_y2"]))], inc=("sg", 1), drain=True)
            S["sv_x_consumed"] = sv_y
            S["sv_y"][tau] = sv_y
            S["sg_bby"][tau] = ctx.cnt["sg"]

        def emit_C(tau):
            par = tau % 2
            C("vector", lambda e: e.drain())
            svr2, r2s, mu2s = ln_stats2(lambda: yt[:], 2, epsln, [("sv", S["sv_y"][tau])])
            for i in range(NI):
                sv_l2b = C("vector", lambda e, i=i, par=par, m=mu2s[i], r=r2s[i]: e.tensor_scalar(
                    out=l2b[:, (par * 2 + i) * D:(par * 2 + i + 1) * D],
                    in0=yt[:, i * D:(i + 1) * D],
                    scalar1=m, scalar2=r, op0=Alu.subtract, op1=Alu.mult),
                    waits=[("sv", svr2), ("sten", S["sten_hT"])], inc=("sv", 1))
                S["tr_l2f"][(tau, i)] = TR.dma("sync", lambda e, i=i, par=par: e.dma_start(
                    out=l2f[:].rearrange("p (k t) -> p k t", k=KC)[:, :, i * 128:(i + 1) * 128],
                    in_=l2b[:, (par * 2 + i) * D:(par * 2 + i + 1) * D], transpose=True),
                    waits=[("sv", sv_l2b), ("sten", S["sten_proj"])])

        def emit_D(tau):
            st_proj = 0
            scan_prev = dict(S.get("scan_inc", {}))
            S["scan_inc"] = {}
            for jp in range(KC):
                odd = jp % 2
                if odd:
                    pbs = (psF[:, 0:256], psF[:, 512:768])
                    freev = S["psF_free"]
                else:
                    pbs = (psP0[:, 0:256], psP1[:, 0:256])
                    freev = S["psPE_free"]
                st_pj = 0
                st_g1 = 0
                for which, jj in enumerate((jp, jp + 8)):
                    pb = pbs[which]
                    w0 = [S["tr_l2f"][(tau, 0)], S["tr_l2f"][(tau, 1)],
                          ("sv", freev[0]), ("ss", freev[1])]
                    if odd:
                        w0.append(("sten", S["sten_ffn1"]))
                    if tau == 0 and jp == 0 and which == 0:
                        w0.append(("sv", sv_Wb))
                    for k in range(KC):
                        st_pj = C("tensor", lambda e, jj=jj, k=k, pb=pb: e.matmul(
                            out=pb,
                            lhsT=Wb[:, k * 2048 + jj * 128:k * 2048 + (jj + 1) * 128],
                            rhs=l2f[:, k * TT:(k + 1) * TT],
                            start=(k == 0), stop=(k == KC - 1)),
                          waits=w0 if k == 0 else (),
                          inc=(("sten", 1) if k == KC - 1 else None))
                    if which == 0:
                        st_g1 = st_pj
                st_proj = st_pj
                sl = jp % 2
                ss_sig = C("scalar", lambda e, jp=jp, sl=sl, pb=pbs[0]: e.activation(
                    out=sgf[:, sl * TT:(sl + 1) * TT], in_=pb,
                    func=Act.Sigmoid, scale=1.0, bias=bvec[:, jp:jp + 1]),
                    waits=[("sten", st_g1), ("sv", S["sgff"][sl])], inc=("ss", 1))
                ss_c = C("scalar", lambda e, jp=jp, pb=pbs[1]: e.activation(
                    out=cfull[:, jp * TT:(jp + 1) * TT], in_=pb,
                    func=Act.Sigmoid, scale=-1.0, bias=negb[:, jp + 8:jp + 9]),
                    waits=[("sten", st_pj), ("sv", scan_prev.get(jp, 0))], inc=("ss", 1))
                sv_g = C("vector", lambda e, jp=jp, sl=sl, pb=pbs[0]: e.scalar_tensor_tensor(
                    out=gsc[:, sl * TT:(sl + 1) * TT], in0=pb,
                    scalar=b05[:, jp:jp + 1], in1=sgf[:, sl * TT:(sl + 1) * TT],
                    op0=Alu.add, op1=Alu.max),
                    waits=[("ss", ss_sig)], inc=("sv", 1))
                C("vector", lambda e, jp=jp, sl=sl: e.tensor_scalar(
                    out=zsc[:, sl * TT:(sl + 1) * TT],
                    in0=cfull[:, jp * TT:(jp + 1) * TT],
                    scalar1=-1.0, scalar2=1.0, op0=Alu.mult, op1=Alu.add),
                  waits=[("ss", ss_c)], drain=True)
                sv_v = C("vector", lambda e, jp=jp, sl=sl: e.tensor_tensor(
                    out=vfull[:, jp * TT:(jp + 1) * TT],
                    in0=zsc[:, sl * TT:(sl + 1) * TT],
                    in1=gsc[:, sl * TT:(sl + 1) * TT], op=Alu.mult),
                    inc=("sv", 1), drain=True)
                init = 0.0 if tau == 0 else carry[:, jp:jp + 1]
                S["prev_scan"] = C("vector", lambda e, jp=jp, init=init: e.tensor_tensor_scan(
                    out=hf[:, jp * TT:(jp + 1) * TT],
                    data0=cfull[:, jp * TT:(jp + 1) * TT],
                    data1=vfull[:, jp * TT:(jp + 1) * TT], initial=init,
                    op0=Alu.mult, op1=Alu.add),
                    waits=([("sten", S["sten_hT"])] if jp == 0 else ()),
                    inc=("sv", 1), drain=(jp == KC - 1))
                S["scan_inc"][jp] = S["prev_scan"]
                S["sv_v"] = sv_v
                S["sgff"][sl] = sv_g
                if odd:
                    S["psF_free"] = (sv_v, ss_sig)
                else:
                    S["psPE_free"] = (sv_v, ss_sig)
            S["sten_proj"] = st_proj
            C("scalar", lambda e: e.activation(
                out=dmy[:], in_=epsln[:], func=Act.Sqrt, scale=1.0, bias=epsln[:]))
            S["sv_carry"] = C("vector", lambda e: e.tensor_copy(
                out=carry[:],
                in_=hf[:].rearrange("p (k t) -> p k t", k=KC)[:, :, TT - 1:TT].squeeze(-1)),
                inc=("sv", 1), drain=True)

        def emit_E(tau):
            par = tau % 2
            st_hT = 0
            sv_y2 = 0
            for i in range(NI):
                pbk = psE if i == 0 else psG
                freek = "psE_free" if i == 0 else "psG_free"
                w0 = [("sv", S[freek][0]), ("ss", S[freek][1])]
                for half in range(2):
                    C("tensor", lambda e, i=i, half=half, par=par, pbk=pbk: e.matmul(
                        out=pbk[:, half * 512:(half + 1) * 512], lhsT=identb[:],
                        rhs=l2b[:, (par * 2 + i) * D + half * 512:
                                (par * 2 + i) * D + (half + 1) * 512],
                        start=True, stop=False, skip_group_check=True),
                      waits=w0 if half == 0 else ())
                for j in range(KC):
                    st_hT = C("tensor", lambda e, i=i, j=j, pbk=pbk: e.matmul(
                        out=pbk[:, j * 128:(j + 1) * 128],
                        lhsT=hf[:, j * TT + i * 128:j * TT + (i + 1) * 128],
                        rhs=ident[:], is_transpose=True,
                        start=False, stop=(j == KC - 1), skip_group_check=True),
                      waits=([("sv", S["scan_inc"][j])] if (i == 0 or j == 0) else ()),
                      inc=(("sten", 1) if j == KC - 1 else None)) or st_hT
                s6 = bns[:, (4 + i) * 12:(4 + i) * 12 + 12]
                C("vector", lambda e, s6=s6, pbk=pbk: e.bn_stats(out=s6[:, 0:6], in_=pbk[:, 0:512]),
                  waits=[("sten", st_hT)])
                C("vector", lambda e, s6=s6, pbk=pbk: e.bn_stats(out=s6[:, 6:12], in_=pbk[:, 512:1024]),
                  drain=True)
                mvs = mv[:, (4 + i) * 2:(4 + i) * 2 + 2]
                sv_mv = C("vector", lambda e, s6=s6, mvs=mvs: e.bn_aggr(out=mvs, in_=s6),
                          inc=("sv", 1))
                sd = sden[:, 4 + i:5 + i]
                ss_sq = C("scalar", lambda e, sd=sd, mvs=mvs: e.activation(
                    out=sd, in_=mvs[:, 1:2], func=Act.Sqrt, bias=epsat[:], scale=1.0),
                    waits=[("sv", sv_mv)], inc=("ss", 1))
                r3 = rr[:, 4 + i:5 + i]
                C("vector", lambda e, r3=r3, sd=sd: e.reciprocal(out=r3, in_=sd),
                  waits=[("ss", ss_sq)], drain=True)
                sv_n = C("vector", lambda e, mvs=mvs, r3=r3, pbk=pbk: e.tensor_scalar(
                    out=nt[:, 0:D], in0=pbk[:],
                    scalar1=mvs[:, 0:1], scalar2=r3, op0=Alu.subtract, op1=Alu.mult),
                    inc=("sv", 1), drain=True)
                S[freek] = (sv_n, 0)
                C("vector", lambda e: e.tensor_tensor(
                    out=t1t, in0=nt[:, 0:D], in1=gb[:], op=Alu.mult),
                  waits=[("sg", S["sg_bby"][tau])], drain=True)
                sv_y2 = C("vector", lambda e, i=i, par=par: e.tensor_tensor(
                    out=y2t[:, i * D:(i + 1) * D], in0=t1t,
                    in1=bby[:, (par * NI + i) * D:(par * NI + i + 1) * D], op=Alu.add),
                    inc=("sv", 1))
            S["sten_hT"] = st_hT
            S["sv_y2"] = sv_y2
            S["sv_y2_hist"][tau] = sv_y2

        def emit_F(tau):
            C("vector", lambda e: e.drain())
            svr4, r4s, mu4s = ln_stats2(lambda: y2t[:], 2, epsln, [("sv", S["sv_y2_hist"][tau])])
            for i in range(NI):
                C("vector", lambda e, i=i, r=r4s[i], m=mu4s[i]: e.tensor_scalar(
                    out=xn2[:, i * D:(i + 1) * D], in0=y2t[:, i * D:(i + 1) * D],
                    scalar1=m, scalar2=r, op0=Alu.subtract, op1=Alu.mult),
                    waits=[("sv", max(svr4, S["sv_xn2_free"]))], drain=(i == NI - 1))
            qa2 = qsc[:, 256 + 0:256 + 64]
            qiv2 = qsc[:, 256 + 64:256 + 128]
            qi62 = qsc[:, 256 + 128:256 + 192]
            qs22 = qsc[:, 256 + 192:256 + 256]
            C("vector", lambda e: e.tensor_reduce(
                out=qa2, in_=xn2[:].rearrange("p (b k) -> p b k", k=32),
                axis=AX.X, op=Alu.max, apply_absolute_value=True),
              waits=[S["tr_xd2_prev"]], drain=True)
            C("vector", lambda e: e.reciprocal(out=qiv2, in_=qa2), drain=True)
            C("vector", lambda e: e.tensor_scalar(
                out=qi62, in0=qiv2, scalar1=7.5 * 2.0 ** -6, scalar2=(1e30 * 2.0 ** -6),
                op0=Alu.mult, op1=Alu.min))
            C("vector", lambda e: e.tensor_scalar(
                out=qs22, in0=qa2, scalar1=64.0 / 7.5, scalar2=6.4e-29,
                op0=Alu.mult, op1=Alu.max), drain=True)
            sv_q82 = C("vector", lambda e: e.tensor_tensor(
                out=q82[:].rearrange("p (b k) -> p b k", k=32),
                in0=xn2[:].rearrange("p (b k) -> p b k", k=32),
                in1=qi62.unsqueeze(-1).broadcast_to([128, 64, 32]),
                op=Alu.mult), inc=("sv", 1), drain=True)
            sv_xd2 = 0
            for i in range(NI):
                hi = slice(i * 32, (i + 1) * 32)
                sv_xd2 = C("vector", lambda e, i=i, hi=hi: e.tensor_tensor(
                    out=xdt2[:, i * D:(i + 1) * D].rearrange("p (b k) -> p b k", k=32),
                    in0=q82[:, i * D:(i + 1) * D].rearrange("p (b k) -> p b k", k=32),
                    in1=qs22[:, hi].unsqueeze(-1).broadcast_to([128, 32, 32]),
                    op=Alu.mult), inc=("sv", 1), drain=(i == NI - 1))
                for hh in range(2):
                    S["tr_xd2"][(tau, i, hh)] = TR.dma("sync", lambda e, i=i, hh=hh: e.dma_start(
                        out=xdf2[:].rearrange("p (k t) -> p k t", k=KC)[
                            :, hh * 4:(hh + 1) * 4, i * 128:(i + 1) * 128],
                        in_=xdt2[:, i * D + hh * 512:i * D + (hh + 1) * 512], transpose=True),
                        waits=[("sv", sv_xd2), ("sten", S["sten_ffn2"])])
            S["sv_xn2_free"] = sv_q82
            S["tr_xd2_prev"] = S["tr_xd2"][(tau, NI - 1, 1)]

        def emit_G_pe(tau):
            st_mmG = {}
            for i in range(NI):
                halves = ([psP0[:], psP1[:]] if i == 0
                          else [psG[:, 0:512], psG[:, 512:1024]])
                freek = "psPE_free" if i == 0 else "psG_free"
                mm_wait0 = [S["tr_xd2"][(tau, i, 0)]]
                mm_wait1 = [S["tr_xd2"][(tau, i, 1)]]
                for half in range(2):
                    C("tensor", lambda e, half=half, halves=halves: e.matmul(
                        out=halves[half], lhsT=ones1[:],
                        rhs=bfh[:, half * 512:(half + 1) * 512], start=True, stop=False),
                      waits=[("sv", max(S[freek][0], sv_bfh)), ("ss", S[freek][1])]
                      if half == 0 else ())
                for half in range(2):
                    for k in range(KC):
                        st_mmG[i] = C("tensor", lambda e, i=i, half=half, k=k, halves=halves: e.matmul(
                            out=halves[half],
                            lhsT=xdf2[:, k * TT + i * 128:k * TT + (i + 1) * 128],
                            rhs=wdT[:, k * 1024 + half * 512:k * 1024 + (half + 1) * 512],
                            start=False, stop=(k == KC - 1)),
                          waits=(mm_wait0 if (half == 0 and k == 0)
                                 else (mm_wait1 if (half == 0 and k == 4) else ())),
                          inc=(("sten", 1) if (half == 1 and k == KC - 1) else None))
            S["st_mmG"] = st_mmG
            S["sten_ffn2"] = st_mmG[NI - 1]

        def emit_G_evac(tau, which):
            for i in which:
                halves = ([psP0[:], psP1[:]] if i == 0
                          else [psG[:, 0:512], psG[:, 512:1024]])
                freek = "psPE_free" if i == 0 else "psG_free"
                sv_out = 0
                for half in range(2):
                    sv_out = C("vector", lambda e, i=i, half=half, halves=halves: e.tensor_tensor(
                        out=outt[:, half * 512:(half + 1) * 512], in0=halves[half],
                        in1=y2t[:, i * D + half * 512:i * D + (half + 1) * 512], op=Alu.add),
                        waits=([("sten", S["st_mmG"][i]), S["store_prev"]] if half == 0 else ()),
                        inc=("sv", 1))
                S[freek] = (sv_out, 0)
                r0 = tau * TT + i * 128
                S["store_prev"] = STP.dma("scalar", lambda e, r0=r0: e.dma_start(
                    out=out_h[r0:r0 + 128, :], in_=outt[:, 0:D]),
                    waits=[("sv", sv_out)])

        emit_A(0)
        emit_B_pe(0)
        cast_done = {}
        wstages = [Wst[:], xn1[:], yt[:], l2b[:].bitcast(F32)]
        for k in range(KC):
            stg = wstages[k % 4]
            ldWk = LD.dma("scalar", lambda e, k=k, stg=stg: e.dma_start(
                out=stg, in_=W_h[k * 128:(k + 1) * 128, :]),
                waits=[("sv", max(cast_done.get(k - 4, 0),
                                  S["sv_xn1_free"] if k % 4 == 1 else 0))])
            cast_done[k] = C("vector", lambda e, k=k, stg=stg: e.tensor_copy(
                out=Wb[:, k * 2048:(k + 1) * 2048], in_=stg),
                waits=[ldWk], inc=("sv", 1), drain=True)
        sv_Wb = cast_done[KC - 1]
        S["sv_xn1_free"] = cast_done[5]
        emit_B_evac(0)
        emit_C(0)
        emit_A_load(1)
        emit_A(1)
        for tau in range(NTAU):
            emit_D(tau)
            emit_E(tau)
            emit_F(tau)
            if tau + 1 < NTAU:
                emit_B_pe(tau + 1)
            emit_G_pe(tau)
            if tau + 1 < NTAU:
                emit_B_evac(tau + 1)
                emit_A_load(tau + 2) if tau + 2 < NTAU else None
                emit_C(tau + 1)
            emit_G_evac(tau, (0,))
            if tau + 2 < NTAU:
                emit_A(tau + 2)
            emit_G_evac(tau, (1,))

        C("gpsimd", lambda e: None, waits=[S["store_prev"]])

        @block.sync
        def _(eng):
            for f in ctx.progs["sync"]:
                f(eng, sems)

        @block.vector
        def _(eng):
            for f in ctx.progs["vector"]:
                f(eng, sems)

        @block.scalar
        def _(eng):
            for f in ctx.progs["scalar"]:
                f(eng, sems)

        @block.tensor
        def _(eng):
            for f in ctx.progs["tensor"]:
                f(eng, sems)

        @block.gpsimd
        def _(eng):
            for f in ctx.progs["gpsimd"]:
                f(eng, sems)

    return nc


_NC_CACHE = {}


def kernel(L0_attn_x, L0_attn_W, L0_attn_b, L0_attn_gamma, L0_attn_beta,
           L0_ffn_weight, L0_ffn_bias):
    x = np.ascontiguousarray(np.asarray(L0_attn_x, dtype=np.float32))
    W = np.ascontiguousarray(np.asarray(L0_attn_W, dtype=np.float32))
    b = np.ascontiguousarray(np.asarray(L0_attn_b, dtype=np.float32))
    gamma = np.ascontiguousarray(np.asarray(L0_attn_gamma, dtype=np.float32))
    beta = np.ascontiguousarray(np.asarray(L0_attn_beta, dtype=np.float32))
    wf = np.ascontiguousarray(np.asarray(L0_ffn_weight, dtype=np.float32))
    bf_ = np.ascontiguousarray(np.asarray(L0_ffn_bias, dtype=np.float32))
    iden = np.eye(128, dtype=np.float32)

    if "nc" not in _NC_CACHE:
        _NC_CACHE["nc"] = build_nc()
    nc = _NC_CACHE["nc"]
    in_maps = [{"x": x[i], "W": W, "b": b, "gamma": gamma, "beta": beta,
                "wf": wf, "bf": bf_, "iden": iden} for i in range(8)]
    res = run_bass_kernel_spmd(nc, in_maps, core_ids=list(range(8)))
    return np.stack([np.asarray(r["out"]) for r in res.results], axis=0)


# version 38
# speedup vs baseline: 1.3205x; 1.0281x over previous
import contextlib
import numpy as np
import concourse.bass as bass
import concourse.mybir as mybir
from concourse.bass_utils import run_bass_kernel_spmd

F32 = mybir.dt.float32
BF16 = mybir.dt.bfloat16
FP8 = mybir.dt.float8e4
Alu = mybir.AluOpType
Act = mybir.ActivationFunctionType
AX = mybir.AxisListType

T = 2048
D = 1024
TT = 256
NTAU = T // TT
NI = TT // 128
KC = 8
E_LN = 1e-5
E_ATT = 1e-6
N_LD = 8
N_TR = 10
N_ST = 4


class Ctx:
    def __init__(self):
        self.cnt = {}
        self.progs = {"sync": [], "vector": [], "scalar": [], "tensor": [], "gpsimd": []}

    def op(self, engine, fn, waits=(), inc=None, drain=False):
        after = None
        if inc is not None:
            sem, amt = inc
            self.cnt[sem] = self.cnt.get(sem, 0) + amt
            after = self.cnt[sem]
        waits = tuple((s, v) for (s, v) in waits if s is not None and v > 0)

        def emit(eng, sems, fn=fn, waits=waits, inc=inc, drain=drain):
            for s, v in waits:
                eng.wait_ge(sems[s], v)
            instr = fn(eng)
            if inc is not None:
                if instr is None:
                    instr = eng.nop()
                instr.then_inc(sems[inc[0]], inc[1])
            if drain:
                eng.drain()
        self.progs[engine].append(emit)
        return after


class DmaPool:
    def __init__(self, ctx, name, nslots):
        self.ctx, self.name, self.n = ctx, name, nslots
        self.uses = [0] * nslots
        self.rr = 0

    def dma(self, queue, fn, waits=()):
        s = self.rr
        self.rr = (self.rr + 1) % self.n
        semname = "%s%d" % (self.name, s)
        w = list(waits)
        if self.uses[s] > 0:
            w.append((semname, 16 * self.uses[s]))
        self.uses[s] += 1
        val = self.ctx.op(queue, fn, waits=w, inc=(semname, 16))
        return (semname, val)


NONE_W = ("sv", 0)


def build_nc():
    nc = bass.Bass("TRN2", target_bir_lowering=False)
    x_h = nc.dram_tensor("x", [T, D], F32, kind="ExternalInput").ap()
    W_h = nc.dram_tensor("W", [D, 2 * D], F32, kind="ExternalInput").ap()
    b_h = nc.dram_tensor("b", [2 * D], F32, kind="ExternalInput").ap()
    gamma_h = nc.dram_tensor("gamma", [D], F32, kind="ExternalInput").ap()
    beta_h = nc.dram_tensor("beta", [D], F32, kind="ExternalInput").ap()
    wf_h = nc.dram_tensor("wf", [D, D], F32, kind="ExternalInput").ap()
    bf_h = nc.dram_tensor("bf", [D], F32, kind="ExternalInput").ap()
    iden_h = nc.dram_tensor("iden", [128, 128], F32, kind="ExternalInput").ap()
    out_h = nc.dram_tensor("out", [T, D], F32, kind="ExternalOutput").ap()

    ctx = Ctx()
    C = ctx.op
    LD = DmaPool(ctx, "ld", N_LD)
    TR = DmaPool(ctx, "tr", N_TR)
    STP = DmaPool(ctx, "st", N_ST)

    es = contextlib.ExitStack()
    with es:
        Wb = es.enter_context(nc.sbuf_tensor("Wb", [128, KC * 2048], BF16))
        wdT = es.enter_context(nc.sbuf_tensor("wdT", [128, KC * 1024], BF16))
        gb = es.enter_context(nc.sbuf_tensor("gb", [128, D], F32))
        bb = es.enter_context(nc.sbuf_tensor("bb", [128, D], F32))
        ident = es.enter_context(nc.sbuf_tensor("ident", [128, 128], F32))
        identb = es.enter_context(nc.sbuf_tensor("identb", [128, 128], BF16))
        ones1 = es.enter_context(nc.sbuf_tensor("ones1", [1, 128], F32))
        grow = es.enter_context(nc.sbuf_tensor("grow", [1, 2 * D], F32))
        bfrow = es.enter_context(nc.sbuf_tensor("bfrow", [1, D], F32))
        bfh = es.enter_context(nc.sbuf_tensor("bfh", [1, D], F32))
        bvec = es.enter_context(nc.sbuf_tensor("bvec", [128, 16], F32))
        negb = es.enter_context(nc.sbuf_tensor("negb", [128, 16], F32))
        b05 = es.enter_context(nc.sbuf_tensor("b05", [128, 16], F32))
        epsln = es.enter_context(nc.sbuf_tensor("epsln", [128, 1], F32))
        epsat = es.enter_context(nc.sbuf_tensor("epsat", [128, 1], F32))
        Wst = es.enter_context(nc.sbuf_tensor("Wst", [128, 2048], F32))
        wfst = es.enter_context(nc.sbuf_tensor("wfst", [128, 1024], F32))
        wq8 = es.enter_context(nc.sbuf_tensor("wq8", [128, 1024], FP8))
        wdtok = es.enter_context(nc.sbuf_tensor("wdtok", [128, 1024], BF16))
        wsc = es.enter_context(nc.sbuf_tensor("wsc", [128, 128], F32))
        xt = es.enter_context(nc.sbuf_tensor("xt", [128, NI * D], F32))
        xn1 = es.enter_context(nc.sbuf_tensor("xn1", [128, 2 * D], F32))
        xn2 = es.enter_context(nc.sbuf_tensor("xn2", [128, 2 * D], F32))
        q81 = es.enter_context(nc.sbuf_tensor("q81", [128, 2 * D], FP8))
        q82 = q81
        xdt1 = es.enter_context(nc.sbuf_tensor("xdt1", [128, 2 * D], BF16))
        xdt2 = es.enter_context(nc.sbuf_tensor("xdt2", [128, 2 * D], BF16))
        xdf1 = es.enter_context(nc.sbuf_tensor("xdf1", [128, KC * TT], BF16))
        xdf2 = es.enter_context(nc.sbuf_tensor("xdf2", [128, KC * TT], BF16))
        yt = es.enter_context(nc.sbuf_tensor("yt", [128, NI * D], F32))
        y2t = Wst[:, 0:2 * D]
        t1t = wfst[:, 0:D]
        bby = es.enter_context(nc.sbuf_tensor("bby", [128, 2 * NI * D], F32))
        l2b = es.enter_context(nc.sbuf_tensor("l2b", [128, 2 * 2 * D], BF16))
        l2f = es.enter_context(nc.sbuf_tensor("l2f", [128, KC * TT], BF16))
        cfull = es.enter_context(nc.sbuf_tensor("cfull", [128, KC * TT], F32))
        vfull = es.enter_context(nc.sbuf_tensor("vfull", [128, KC * TT], F32))
        gsc = es.enter_context(nc.sbuf_tensor("gsc", [128, 2 * TT], F32))
        sgf = es.enter_context(nc.sbuf_tensor("sgf", [128, 2 * TT], F32))
        zsc = wdtok[:].bitcast(F32)
        hf = es.enter_context(nc.sbuf_tensor("hf", [128, KC * TT], F32))
        carry = es.enter_context(nc.sbuf_tensor("carry", [128, KC], F32))
        nt = es.enter_context(nc.sbuf_tensor("nt", [128, D], F32))
        outt = es.enter_context(nc.sbuf_tensor("outt", [128, D], F32))
        bns = es.enter_context(nc.sbuf_tensor("bns", [128, 72], F32))
        mv = es.enter_context(nc.sbuf_tensor("mv", [128, 12], F32))
        sden = es.enter_context(nc.sbuf_tensor("sden", [128, 6], F32))
        rr = es.enter_context(nc.sbuf_tensor("rr", [128, 6], F32))
        mb = es.enter_context(nc.sbuf_tensor("mb", [128, 6], F32))
        qsc = es.enter_context(nc.sbuf_tensor("qsc", [128, 512], F32))
        dmy = es.enter_context(nc.sbuf_tensor("dmy", [128, 1], F32))
        psF = es.enter_context(nc.psum_tensor("psF", [128, 1024], F32))
        psG = es.enter_context(nc.psum_tensor("psG", [128, 1024], F32))
        psE = es.enter_context(nc.psum_tensor("psE", [128, 1024], F32))
        psP0 = es.enter_context(nc.psum_tensor("psP0", [128, 512], F32))
        psP1 = es.enter_context(nc.psum_tensor("psP1", [128, 512], F32))
        sems = {}
        for nm in (["sv", "ss", "sten", "sg"]
                   + ["ld%d" % i for i in range(N_LD)]
                   + ["tr%d" % i for i in range(N_TR)]
                   + ["st%d" % i for i in range(N_ST)]):
            sems[nm] = es.enter_context(nc.semaphore(nm))
        block = es.enter_context(nc.Block())
        psP = [psP0, psP1]

        ld_iden = LD.dma("sync", lambda e: e.dma_start(out=ident[:], in_=iden_h))
        ld_grow = LD.dma("sync", lambda e: e.dma_start(out=grow[:, 0:D], in_=gamma_h.unsqueeze(0)))
        ld_brow = LD.dma("sync", lambda e: e.dma_start(out=grow[:, D:2 * D], in_=beta_h.unsqueeze(0)))
        ld_bfrow = LD.dma("sync", lambda e: e.dma_start(out=bfrow[:], in_=bf_h.unsqueeze(0)))
        ld_bvec = LD.dma("sync", lambda e: e.dma_start(
            out=bvec[:], in_=b_h.rearrange("(j p) -> p j", p=128),
            allow_slow_non_contiguous=True))

        C("vector", lambda e: e.memset(epsln[:], E_LN))
        C("vector", lambda e: e.memset(epsat[:], E_ATT))
        C("vector", lambda e: e.memset(ones1[:], 1.0))
        sv_ones = C("vector", lambda e: e.tensor_copy(out=identb[:], in_=ident[:]),
                    waits=[ld_iden], inc=("sv", 1))

        prev_q = 0
        prev_wtr = NONE_W
        wfq_done = {}
        init_ld_x0 = None
        for k in range(KC):
            if k == 2:
                init_ld_x0 = LD.dma("scalar", lambda e: e.dma_start(
                    out=xt[:].rearrange("p (i c) -> p i c", i=NI),
                    in_=x_h[0:TT, :].rearrange("(i p) c -> p i c", p=128)))
            wstg = wfst[:, 0:D] if k % 2 == 0 else xn2[:, 0:D]
            ldwf = LD.dma("scalar", lambda e, k=k, wstg=wstg: e.dma_start(
                out=wstg, in_=wf_h[k * 128:(k + 1) * 128, :]),
                waits=[("sv", wfq_done.get(k - 2, 0))])
            wam, wscl, wi6, ws2 = (wsc[:, 0:32], wsc[:, 32:64], wsc[:, 64:96], wsc[:, 96:128])
            C("vector", lambda e, wstg=wstg: e.tensor_reduce(
                out=wam, in_=wstg.rearrange("p (b k) -> p b k", k=32),
                axis=AX.X, op=Alu.max, apply_absolute_value=True),
              waits=[ldwf], drain=True)
            C("vector", lambda e: e.tensor_scalar(
                out=wscl, in0=wam, scalar1=1.0 / 7.5, scalar2=1e-30,
                op0=Alu.mult, op1=Alu.max), drain=True)
            C("vector", lambda e: e.reciprocal(out=wi6, in_=wscl), drain=True)
            C("vector", lambda e: e.tensor_scalar(
                out=wi6, in0=wi6, scalar1=2.0 ** -6, scalar2=None, op0=Alu.mult), drain=True)
            C("vector", lambda e: e.tensor_scalar(
                out=ws2, in0=wscl, scalar1=32.0, scalar2=None, op0=Alu.mult), drain=True)
            wfq_done[k] = C("vector", lambda e, wstg=wstg: e.tensor_tensor(
                out=wq8[:].rearrange("p (b k) -> p b k", k=32),
                in0=wstg.rearrange("p (b k) -> p b k", k=32),
                in1=wi6.unsqueeze(-1).broadcast_to([128, 32, 32]), op=Alu.mult),
                inc=("sv", 1), drain=True)
            prev_q = C("vector", lambda e: e.tensor_tensor(
                out=wdtok[:].rearrange("p (b k) -> p b k", k=32),
                in0=wq8[:].rearrange("p (b k) -> p b k", k=32),
                in1=ws2.unsqueeze(-1).broadcast_to([128, 32, 32]), op=Alu.mult),
                waits=[prev_wtr], inc=("sv", 1), drain=True)
            prev_wtr = TR.dma("sync", lambda e, k=k: e.dma_start(
                out=wdT[:].rearrange("p (m c l) -> p m c l", m=KC, c=KC)[:, :, k, :],
                in_=wdtok[:], transpose=True),
                waits=[("sv", prev_q)])
        tr_wdT = prev_wtr

        C("vector", lambda e: e.tensor_scalar(out=negb[:], in0=bvec[:], scalar1=-1.0,
                                              scalar2=None, op0=Alu.mult),
          waits=[ld_bvec])
        sv_bvecs = C("vector", lambda e: e.tensor_scalar(out=b05[:], in0=bvec[:], scalar1=0.5,
                                                         scalar2=None, op0=Alu.add),
                     inc=("sv", 1))
        sv_bfh = C("vector", lambda e: e.tensor_scalar(out=bfh[:], in0=bfrow[:], scalar1=0.5,
                                                       scalar2=None, op0=Alu.mult),
                   waits=[ld_bfrow], inc=("sv", 1), drain=True)

        st_gbmm = 0
        for half in range(2):
            C("tensor", lambda e, h=half: e.matmul(
                out=psF[:, h * 512:(h + 1) * 512], lhsT=ones1[:],
                rhs=grow[:, h * 512:(h + 1) * 512], start=True, stop=True),
              waits=([ld_grow, ld_iden, ld_brow, ("sv", sv_ones)] if half == 0 else ()))
            st_gbmm = C("tensor", lambda e, h=half: e.matmul(
                out=psG[:, h * 512:(h + 1) * 512], lhsT=ones1[:],
                rhs=grow[:, D + h * 512:D + (h + 1) * 512], start=True, stop=True),
                inc=(("sten", 1) if half == 1 else None)) or st_gbmm
        C("vector", lambda e: e.tensor_copy(out=gb[:], in_=psF[:]),
          waits=[("sten", st_gbmm)])
        sv_gb = C("vector", lambda e: e.tensor_copy(out=bb[:], in_=psG[:]),
                  inc=("sv", 1), drain=True)

        def ln_stats2(src2_getter, base_slot, eps_tile, wait_list):
            s6a = bns[:, base_slot * 12:base_slot * 12 + 12]
            s6b = bns[:, (base_slot + 1) * 12:(base_slot + 1) * 12 + 12]
            C("vector", lambda e: e.bn_stats(out=s6a[:, 0:6], in_=src2_getter()[:, 0:512]),
              waits=wait_list)
            C("vector", lambda e: e.bn_stats(out=s6a[:, 6:12], in_=src2_getter()[:, 512:1024]))
            C("vector", lambda e: e.bn_stats(out=s6b[:, 0:6], in_=src2_getter()[:, 1024:1536]))
            C("vector", lambda e: e.bn_stats(out=s6b[:, 6:12], in_=src2_getter()[:, 1536:2048]),
              drain=True)
            mv2 = mv[:, base_slot * 2:base_slot * 2 + 4]
            C("vector", lambda e: e.bn_aggr(out=mv2[:, 0:2], in_=s6a))
            sv_mv = C("vector", lambda e: e.bn_aggr(out=mv2[:, 2:4], in_=s6b),
                      inc=("sv", 1))
            sd = sden[:, base_slot:base_slot + 2]
            varap = mv2.rearrange("p (s c) -> p s c", c=2)[:, :, 1:2].squeeze(-1)
            ss_sq = C("scalar", lambda e: e.activation(
                out=sd, in_=varap, func=Act.Sqrt, bias=eps_tile[:], scale=1.0),
                waits=[("sv", sv_mv)], inc=("ss", 1))
            r2ap = rr[:, base_slot:base_slot + 2]
            sv_r = C("vector", lambda e: e.reciprocal(out=r2ap, in_=sd),
                     waits=[("ss", ss_sq)], inc=("sv", 1), drain=True)
            rs = [rr[:, base_slot + i:base_slot + 1 + i] for i in range(2)]
            mus = [mv[:, base_slot * 2 + 2 * i:base_slot * 2 + 2 * i + 1] for i in range(2)]
            return sv_r, rs, mus

        def mkmb(base_slot, sv_r):
            mv2mu = mv[:, base_slot * 2:base_slot * 2 + 4].rearrange(
                "p (s c) -> p s c", c=2)[:, :, 0:1].squeeze(-1)
            m2ap = mb[:, base_slot:base_slot + 2]
            sv_m = C("vector", lambda e: e.scalar_tensor_tensor(
                out=m2ap, in0=mv2mu, scalar=-1.0, in1=rr[:, base_slot:base_slot + 2],
                op0=Alu.mult, op1=Alu.mult), inc=("sv", 1))
            return sv_m, [mb[:, base_slot + i:base_slot + 1 + i] for i in range(2)]

        S = dict(
            psF_free=(sv_gb, 0), psG_free=(sv_gb, 0), psE_free=(0, 0),
            psPE_free=(0, 0),
            sv_x_consumed=0, sten_hT=0, sten_ffn1=0, sten_ffn2=0, sten_proj=0,
            tr_xd1_prev=NONE_W, tr_xd2_prev=NONE_W,
            sv_xn1_free=0, sv_xn2_free=wfq_done[KC - 1],
            prev_scan=0, sv_carry=0, gscf={0: 0, 1: 0}, sgff={0: 0, 1: 0},
            sg_bby={}, sv_y={}, sv_y2=0, sv_y2_hist={},
            ld_x={}, tr_xdf={}, tr_xd2={}, tr_l2f={}, store_prev=NONE_W,
        )
        S["ld_x"][(0, 0)] = init_ld_x0
        S["ld_x"][(0, 1)] = init_ld_x0

        def emit_A_load(tau):
            r0 = tau * TT
            ld = LD.dma("scalar", lambda e, r0=r0: e.dma_start(
                out=xt[:].rearrange("p (i c) -> p i c", i=NI),
                in_=x_h[r0:r0 + TT, :].rearrange("(i p) c -> p i c", p=128)),
                waits=[("sv", S["sv_x_consumed"])])
            for i in range(NI):
                S["ld_x"][(tau, i)] = ld

        def emit_A(tau, defer=False):
            svr, r1s, mu1s = ln_stats2(lambda: xt[:], 0, epsln,
                                       [S["ld_x"][(tau, 0)], S["ld_x"][(tau, 1)]])
            for i in range(NI):
                C("vector", lambda e, i=i, r=r1s[i], m=mu1s[i]: e.tensor_scalar(
                    out=xn1[:, i * D:(i + 1) * D], in0=xt[:, i * D:(i + 1) * D],
                    scalar1=m, scalar2=r, op0=Alu.subtract, op1=Alu.mult),
                    waits=[("sv", max(svr, S["sv_xn1_free"]))], drain=(i == NI - 1))
            C("scalar", lambda e: e.activation(
                out=dmy[:], in_=epsln[:], func=Act.Sigmoid, scale=1.0, bias=0.0))
            qa = qsc[:, 0:64]
            qiv = qsc[:, 64:128]
            qi6 = qsc[:, 128:192]
            qs2 = qsc[:, 192:256]

            def th_amax():
                C("vector", lambda e: e.tensor_reduce(
                    out=qa, in_=xn1[:].rearrange("p (b k) -> p b k", k=32),
                    axis=AX.X, op=Alu.max, apply_absolute_value=True),
                  waits=[S["tr_xd1_prev"]], drain=True)

            def th_scales():
                C("vector", lambda e: e.reciprocal(out=qiv, in_=qa), drain=True)
                C("vector", lambda e: e.tensor_scalar(
                    out=qi6, in0=qiv, scalar1=7.5 * 2.0 ** -6, scalar2=(1e30 * 2.0 ** -6),
                    op0=Alu.mult, op1=Alu.min))
                C("vector", lambda e: e.tensor_scalar(
                    out=qs2, in0=qa, scalar1=64.0 / 7.5, scalar2=6.4e-29,
                    op0=Alu.mult, op1=Alu.max), drain=True)

            def th_q8():
                S["_svq8"] = C("vector", lambda e: e.tensor_tensor(
                    out=q81[:].rearrange("p (b k) -> p b k", k=32),
                    in0=xn1[:].rearrange("p (b k) -> p b k", k=32),
                    in1=qi6.unsqueeze(-1).broadcast_to([128, 64, 32]),
                    op=Alu.mult), inc=("sv", 1), drain=True)

            def mk_th_xd(i):
                def th():
                    hi = slice(i * 32, (i + 1) * 32)
                    sv_xd = C("vector", lambda e, i=i, hi=hi: e.tensor_tensor(
                        out=xdt1[:, i * D:(i + 1) * D].rearrange("p (b k) -> p b k", k=32),
                        in0=q81[:, i * D:(i + 1) * D].rearrange("p (b k) -> p b k", k=32),
                        in1=qs2[:, hi].unsqueeze(-1).broadcast_to([128, 32, 32]),
                        op=Alu.mult), inc=("sv", 1), drain=True)
                    for hh in range(2):
                        S["tr_xdf"][(tau, i, hh)] = TR.dma("sync", lambda e, i=i, hh=hh: e.dma_start(
                            out=xdf1[:].rearrange("p (k t) -> p k t", k=KC)[
                                :, hh * 4:(hh + 1) * 4, i * 128:(i + 1) * 128],
                            in_=xdt1[:, i * D + hh * 512:i * D + (hh + 1) * 512], transpose=True),
                            waits=[("sv", sv_xd), ("sten", S["sten_ffn1"])])
                    if i == NI - 1:
                        S["sv_xn1_free"] = S["_svq8"]
                        S["tr_xd1_prev"] = S["tr_xdf"][(tau, NI - 1, 1)]
                return th

            ths = [th_amax, th_scales, th_q8, mk_th_xd(0), mk_th_xd(1)]
            if defer:
                S["a_ths"] = ths
            else:
                for t in ths:
                    t()

        def emit_B_pe(tau):
            st_mmB = {}
            for i in range(NI):
                pbk = psF if i == 0 else psE
                freek = "psF_free" if i == 0 else "psE_free"
                mm_wait0 = [S["tr_xdf"][(tau, i, 0)], S["ld_x"][(tau, i)]]
                mm_wait1 = [S["tr_xdf"][(tau, i, 1)]]
                if tau == 0 and i == 0:
                    mm_wait0.append(tr_wdT)
                for half in range(2):
                    C("tensor", lambda e, half=half, pbk=pbk: e.matmul(
                        out=pbk[:, half * 512:(half + 1) * 512], lhsT=ones1[:],
                        rhs=bfh[:, half * 512:(half + 1) * 512], start=True, stop=False),
                      waits=[("sv", max(S[freek][0], sv_bfh)), ("ss", S[freek][1])]
                      if half == 0 else ())
                for half in range(2):
                    for k in range(KC):
                        st_mmB[i] = C("tensor", lambda e, i=i, half=half, k=k, pbk=pbk: e.matmul(
                            out=pbk[:, half * 512:(half + 1) * 512],
                            lhsT=xdf1[:, k * TT + i * 128:k * TT + (i + 1) * 128],
                            rhs=wdT[:, k * 1024 + half * 512:k * 1024 + (half + 1) * 512],
                            start=False, stop=(k == KC - 1)),
                          waits=(mm_wait0 if (half == 0 and k == 0)
                                 else (mm_wait1 if (half == 0 and k == 4) else ())),
                          inc=(("sten", 1) if (half == 1 and k == KC - 1) else None))
            S["st_mmB"] = st_mmB
            S["sten_ffn1"] = st_mmB[NI - 1]

        def emit_B_evac(tau):
            sv_y = 0
            sg_prev_bby = S["sg_bby"].get(tau - 1, 0)
            for i in range(NI):
                pbk = psF if i == 0 else psE
                freek = "psF_free" if i == 0 else "psE_free"
                sv_y = C("vector", lambda e, i=i, pbk=pbk: e.tensor_tensor(
                    out=yt[:, i * D:(i + 1) * D], in0=pbk[:],
                    in1=xt[:, i * D:(i + 1) * D], op=Alu.add),
                    waits=[("sten", S["st_mmB"][i]), ("sg", sg_prev_bby)], inc=("sv", 1))
                S[freek] = (sv_y, 0)
                C("gpsimd", lambda e, i=i, par=tau % 2: e.tensor_tensor(
                    out=bby[:, (par * NI + i) * D:(par * NI + i + 1) * D],
                    in0=yt[:, i * D:(i + 1) * D], in1=bb[:], op=Alu.add),
                    waits=[("sv", max(sv_y, sv_gb, S["sv_y2"]))], inc=("sg", 1), drain=True)
            S["sv_x_consumed"] = sv_y
            S["sv_y"][tau] = sv_y
            S["sg_bby"][tau] = ctx.cnt["sg"]

        def emit_C(tau):
            par = tau % 2
            C("vector", lambda e: e.drain())
            svr2, r2s, mu2s = ln_stats2(lambda: yt[:], 2, epsln, [("sv", S["sv_y"][tau])])
            for i in range(NI):
                sv_l2b = C("vector", lambda e, i=i, par=par, m=mu2s[i], r=r2s[i]: e.tensor_scalar(
                    out=l2b[:, (par * 2 + i) * D:(par * 2 + i + 1) * D],
                    in0=yt[:, i * D:(i + 1) * D],
                    scalar1=m, scalar2=r, op0=Alu.subtract, op1=Alu.mult),
                    waits=[("sv", svr2), ("sten", S["sten_hT"])], inc=("sv", 1))
                S["tr_l2f"][(tau, i)] = TR.dma("sync", lambda e, i=i, par=par: e.dma_start(
                    out=l2f[:].rearrange("p (k t) -> p k t", k=KC)[:, :, i * 128:(i + 1) * 128],
                    in_=l2b[:, (par * 2 + i) * D:(par * 2 + i + 1) * D], transpose=True),
                    waits=[("sv", sv_l2b), ("sten", S["sten_proj"])])

        def emit_D(tau):
            st_proj = 0
            scan_prev = dict(S.get("scan_inc", {}))
            S["scan_inc"] = {}
            a_ths = S.pop("a_ths", [])
            if a_ths:
                a_ths[0]()
            for jp in range(KC):
                odd = jp % 2
                if odd:
                    pbs = (psF[:, 0:256], psF[:, 512:768])
                    freev = S["psF_free"]
                else:
                    pbs = (psP0[:, 0:256], psP1[:, 0:256])
                    freev = S["psPE_free"]
                st_pj = 0
                st_g1 = 0
                for which, jj in enumerate((jp, jp + 8)):
                    pb = pbs[which]
                    w0 = [S["tr_l2f"][(tau, 0)], S["tr_l2f"][(tau, 1)],
                          ("sv", freev[0]), ("ss", freev[1])]
                    if odd:
                        w0.append(("sten", S["sten_ffn1"]))
                    if tau == 0 and jp == 0 and which == 0:
                        w0.append(("sv", sv_Wb))
                    for k in range(KC):
                        st_pj = C("tensor", lambda e, jj=jj, k=k, pb=pb: e.matmul(
                            out=pb,
                            lhsT=Wb[:, k * 2048 + jj * 128:k * 2048 + (jj + 1) * 128],
                            rhs=l2f[:, k * TT:(k + 1) * TT],
                            start=(k == 0), stop=(k == KC - 1)),
                          waits=w0 if k == 0 else (),
                          inc=(("sten", 1) if k == KC - 1 else None))
                    if which == 0:
                        st_g1 = st_pj
                st_proj = st_pj
                sl = jp % 2
                ss_sig = C("scalar", lambda e, jp=jp, sl=sl, pb=pbs[0]: e.activation(
                    out=sgf[:, sl * TT:(sl + 1) * TT], in_=pb,
                    func=Act.Sigmoid, scale=1.0, bias=bvec[:, jp:jp + 1]),
                    waits=[("sten", st_g1), ("sv", S["sgff"][sl])], inc=("ss", 1))
                ss_c = C("scalar", lambda e, jp=jp, pb=pbs[1]: e.activation(
                    out=cfull[:, jp * TT:(jp + 1) * TT], in_=pb,
                    func=Act.Sigmoid, scale=-1.0, bias=negb[:, jp + 8:jp + 9]),
                    waits=[("sten", st_pj), ("sv", scan_prev.get(jp, 0))], inc=("ss", 1))
                sv_g = C("vector", lambda e, jp=jp, sl=sl, pb=pbs[0]: e.scalar_tensor_tensor(
                    out=gsc[:, sl * TT:(sl + 1) * TT], in0=pb,
                    scalar=b05[:, jp:jp + 1], in1=sgf[:, sl * TT:(sl + 1) * TT],
                    op0=Alu.add, op1=Alu.max),
                    waits=[("ss", ss_sig)], inc=("sv", 1))
                C("vector", lambda e, jp=jp, sl=sl: e.tensor_scalar(
                    out=zsc[:, sl * TT:(sl + 1) * TT],
                    in0=cfull[:, jp * TT:(jp + 1) * TT],
                    scalar1=-1.0, scalar2=1.0, op0=Alu.mult, op1=Alu.add),
                  waits=[("ss", ss_c)], drain=True)
                sv_v = C("vector", lambda e, jp=jp, sl=sl: e.tensor_tensor(
                    out=vfull[:, jp * TT:(jp + 1) * TT],
                    in0=zsc[:, sl * TT:(sl + 1) * TT],
                    in1=gsc[:, sl * TT:(sl + 1) * TT], op=Alu.mult),
                    inc=("sv", 1), drain=True)
                init = 0.0 if tau == 0 else carry[:, jp:jp + 1]
                S["prev_scan"] = C("vector", lambda e, jp=jp, init=init: e.tensor_tensor_scan(
                    out=hf[:, jp * TT:(jp + 1) * TT],
                    data0=cfull[:, jp * TT:(jp + 1) * TT],
                    data1=vfull[:, jp * TT:(jp + 1) * TT], initial=init,
                    op0=Alu.mult, op1=Alu.add),
                    waits=([("sten", S["sten_hT"])] if jp == 0 else ()),
                    inc=("sv", 1), drain=(jp == KC - 1))
                S["scan_inc"][jp] = S["prev_scan"]
                S["sv_v"] = sv_v
                S["sgff"][sl] = sv_g
                if odd:
                    S["psF_free"] = (sv_v, ss_sig)
                else:
                    S["psPE_free"] = (sv_v, ss_sig)
                if jp % 2 == 1 and (jp + 1) // 2 < len(a_ths):
                    a_ths[(jp + 1) // 2]()
            S["sten_proj"] = st_proj
            C("scalar", lambda e: e.activation(
                out=dmy[:], in_=epsln[:], func=Act.Sqrt, scale=1.0, bias=epsln[:]))
            S["sv_carry"] = C("vector", lambda e: e.tensor_copy(
                out=carry[:],
                in_=hf[:].rearrange("p (k t) -> p k t", k=KC)[:, :, TT - 1:TT].squeeze(-1)),
                inc=("sv", 1), drain=True)

        def emit_E(tau):
            par = tau % 2
            st_hT = 0
            sv_y2 = 0
            for i in range(NI):
                pbk = psE if i == 0 else psG
                freek = "psE_free" if i == 0 else "psG_free"
                w0 = [("sv", S[freek][0]), ("ss", S[freek][1])]
                for half in range(2):
                    C("tensor", lambda e, i=i, half=half, par=par, pbk=pbk: e.matmul(
                        out=pbk[:, half * 512:(half + 1) * 512], lhsT=identb[:],
                        rhs=l2b[:, (par * 2 + i) * D + half * 512:
                                (par * 2 + i) * D + (half + 1) * 512],
                        start=True, stop=False, skip_group_check=True),
                      waits=w0 if half == 0 else ())
                for j in range(KC):
                    st_hT = C("tensor", lambda e, i=i, j=j, pbk=pbk: e.matmul(
                        out=pbk[:, j * 128:(j + 1) * 128],
                        lhsT=hf[:, j * TT + i * 128:j * TT + (i + 1) * 128],
                        rhs=ident[:], is_transpose=True,
                        start=False, stop=(j == KC - 1), skip_group_check=True),
                      waits=([("sv", S["scan_inc"][j])] if (i == 0 or j == 0) else ()),
                      inc=(("sten", 1) if j == KC - 1 else None)) or st_hT
                s6 = bns[:, (4 + i) * 12:(4 + i) * 12 + 12]
                C("vector", lambda e, s6=s6, pbk=pbk: e.bn_stats(out=s6[:, 0:6], in_=pbk[:, 0:512]),
                  waits=[("sten", st_hT)])
                C("vector", lambda e, s6=s6, pbk=pbk: e.bn_stats(out=s6[:, 6:12], in_=pbk[:, 512:1024]),
                  drain=True)
                mvs = mv[:, (4 + i) * 2:(4 + i) * 2 + 2]
                sv_mv = C("vector", lambda e, s6=s6, mvs=mvs: e.bn_aggr(out=mvs, in_=s6),
                          inc=("sv", 1))
                sd = sden[:, 4 + i:5 + i]
                ss_sq = C("scalar", lambda e, sd=sd, mvs=mvs: e.activation(
                    out=sd, in_=mvs[:, 1:2], func=Act.Sqrt, bias=epsat[:], scale=1.0),
                    waits=[("sv", sv_mv)], inc=("ss", 1))
                r3 = rr[:, 4 + i:5 + i]
                C("vector", lambda e, r3=r3, sd=sd: e.reciprocal(out=r3, in_=sd),
                  waits=[("ss", ss_sq)], drain=True)
                sv_n = C("vector", lambda e, mvs=mvs, r3=r3, pbk=pbk: e.tensor_scalar(
                    out=nt[:, 0:D], in0=pbk[:],
                    scalar1=mvs[:, 0:1], scalar2=r3, op0=Alu.subtract, op1=Alu.mult),
                    inc=("sv", 1), drain=True)
                S[freek] = (sv_n, 0)
                C("vector", lambda e: e.tensor_tensor(
                    out=t1t, in0=nt[:, 0:D], in1=gb[:], op=Alu.mult),
                  waits=[("sg", S["sg_bby"][tau])], drain=True)
                sv_y2 = C("vector", lambda e, i=i, par=par: e.tensor_tensor(
                    out=y2t[:, i * D:(i + 1) * D], in0=t1t,
                    in1=bby[:, (par * NI + i) * D:(par * NI + i + 1) * D], op=Alu.add),
                    inc=("sv", 1))
            S["sten_hT"] = st_hT
            S["sv_y2"] = sv_y2
            S["sv_y2_hist"][tau] = sv_y2

        def emit_F(tau):
            C("vector", lambda e: e.drain())
            svr4, r4s, mu4s = ln_stats2(lambda: y2t[:], 2, epsln, [("sv", S["sv_y2_hist"][tau])])
            for i in range(NI):
                C("vector", lambda e, i=i, r=r4s[i], m=mu4s[i]: e.tensor_scalar(
                    out=xn2[:, i * D:(i + 1) * D], in0=y2t[:, i * D:(i + 1) * D],
                    scalar1=m, scalar2=r, op0=Alu.subtract, op1=Alu.mult),
                    waits=[("sv", max(svr4, S["sv_xn2_free"]))], drain=(i == NI - 1))
            qa2 = qsc[:, 256 + 0:256 + 64]
            qiv2 = qsc[:, 256 + 64:256 + 128]
            qi62 = qsc[:, 256 + 128:256 + 192]
            qs22 = qsc[:, 256 + 192:256 + 256]
            C("vector", lambda e: e.tensor_reduce(
                out=qa2, in_=xn2[:].rearrange("p (b k) -> p b k", k=32),
                axis=AX.X, op=Alu.max, apply_absolute_value=True),
              waits=[S["tr_xd2_prev"]], drain=True)
            C("vector", lambda e: e.reciprocal(out=qiv2, in_=qa2), drain=True)
            C("vector", lambda e: e.tensor_scalar(
                out=qi62, in0=qiv2, scalar1=7.5 * 2.0 ** -6, scalar2=(1e30 * 2.0 ** -6),
                op0=Alu.mult, op1=Alu.min))
            C("vector", lambda e: e.tensor_scalar(
                out=qs22, in0=qa2, scalar1=64.0 / 7.5, scalar2=6.4e-29,
                op0=Alu.mult, op1=Alu.max), drain=True)
            sv_q82 = C("vector", lambda e: e.tensor_tensor(
                out=q82[:].rearrange("p (b k) -> p b k", k=32),
                in0=xn2[:].rearrange("p (b k) -> p b k", k=32),
                in1=qi62.unsqueeze(-1).broadcast_to([128, 64, 32]),
                op=Alu.mult), inc=("sv", 1), drain=True)
            sv_xd2 = 0
            for i in range(NI):
                hi = slice(i * 32, (i + 1) * 32)
                sv_xd2 = C("vector", lambda e, i=i, hi=hi: e.tensor_tensor(
                    out=xdt2[:, i * D:(i + 1) * D].rearrange("p (b k) -> p b k", k=32),
                    in0=q82[:, i * D:(i + 1) * D].rearrange("p (b k) -> p b k", k=32),
                    in1=qs22[:, hi].unsqueeze(-1).broadcast_to([128, 32, 32]),
                    op=Alu.mult), inc=("sv", 1), drain=(i == NI - 1))
                for hh in range(2):
                    S["tr_xd2"][(tau, i, hh)] = TR.dma("sync", lambda e, i=i, hh=hh: e.dma_start(
                        out=xdf2[:].rearrange("p (k t) -> p k t", k=KC)[
                            :, hh * 4:(hh + 1) * 4, i * 128:(i + 1) * 128],
                        in_=xdt2[:, i * D + hh * 512:i * D + (hh + 1) * 512], transpose=True),
                        waits=[("sv", sv_xd2), ("sten", S["sten_ffn2"])])
            S["sv_xn2_free"] = sv_q82
            S["tr_xd2_prev"] = S["tr_xd2"][(tau, NI - 1, 1)]

        def emit_G_pe(tau):
            st_mmG = {}
            for i in range(NI):
                halves = ([psP0[:], psP1[:]] if i == 0
                          else [psG[:, 0:512], psG[:, 512:1024]])
                freek = "psPE_free" if i == 0 else "psG_free"
                mm_wait0 = [S["tr_xd2"][(tau, i, 0)]]
                mm_wait1 = [S["tr_xd2"][(tau, i, 1)]]
                for half in range(2):
                    C("tensor", lambda e, half=half, halves=halves: e.matmul(
                        out=halves[half], lhsT=ones1[:],
                        rhs=bfh[:, half * 512:(half + 1) * 512], start=True, stop=False),
                      waits=[("sv", max(S[freek][0], sv_bfh)), ("ss", S[freek][1])]
                      if half == 0 else ())
                for half in range(2):
                    for k in range(KC):
                        st_mmG[i] = C("tensor", lambda e, i=i, half=half, k=k, halves=halves: e.matmul(
                            out=halves[half],
                            lhsT=xdf2[:, k * TT + i * 128:k * TT + (i + 1) * 128],
                            rhs=wdT[:, k * 1024 + half * 512:k * 1024 + (half + 1) * 512],
                            start=False, stop=(k == KC - 1)),
                          waits=(mm_wait0 if (half == 0 and k == 0)
                                 else (mm_wait1 if (half == 0 and k == 4) else ())),
                          inc=(("sten", 1) if (half == 1 and k == KC - 1) else None))
            S["st_mmG"] = st_mmG
            S["sten_ffn2"] = st_mmG[NI - 1]

        def emit_G_evac(tau, which):
            for i in which:
                halves = ([psP0[:], psP1[:]] if i == 0
                          else [psG[:, 0:512], psG[:, 512:1024]])
                freek = "psPE_free" if i == 0 else "psG_free"
                sv_out = 0
                for half in range(2):
                    sv_out = C("vector", lambda e, i=i, half=half, halves=halves: e.tensor_tensor(
                        out=outt[:, half * 512:(half + 1) * 512], in0=halves[half],
                        in1=y2t[:, i * D + half * 512:i * D + (half + 1) * 512], op=Alu.add),
                        waits=([("sten", S["st_mmG"][i]), S["store_prev"]] if half == 0 else ()),
                        inc=("sv", 1))
                S[freek] = (sv_out, 0)
                r0 = tau * TT + i * 128
                S["store_prev"] = STP.dma("scalar", lambda e, r0=r0: e.dma_start(
                    out=out_h[r0:r0 + 128, :], in_=outt[:, 0:D]),
                    waits=[("sv", sv_out)])

        emit_A(0)
        emit_B_pe(0)
        cast_done = {}
        wstages = [Wst[:], xn1[:], yt[:], l2b[:].bitcast(F32)]
        for k in range(KC):
            stg = wstages[k % 4]
            ldWk = LD.dma("scalar", lambda e, k=k, stg=stg: e.dma_start(
                out=stg, in_=W_h[k * 128:(k + 1) * 128, :]),
                waits=[("sv", max(cast_done.get(k - 4, 0),
                                  S["sv_xn1_free"] if k % 4 == 1 else 0))])
            cast_done[k] = C("vector", lambda e, k=k, stg=stg: e.tensor_copy(
                out=Wb[:, k * 2048:(k + 1) * 2048], in_=stg),
                waits=[ldWk], inc=("sv", 1), drain=True)
        sv_Wb = cast_done[KC - 1]
        S["sv_xn1_free"] = cast_done[5]
        emit_B_evac(0)
        emit_C(0)
        emit_A_load(1)
        emit_A(1)
        for tau in range(NTAU):
            emit_D(tau)
            emit_E(tau)
            emit_F(tau)
            if tau + 1 < NTAU:
                emit_B_pe(tau + 1)
            emit_G_pe(tau)
            if tau + 1 < NTAU:
                emit_B_evac(tau + 1)
                emit_A_load(tau + 2) if tau + 2 < NTAU else None
                emit_C(tau + 1)
            emit_G_evac(tau, (0,))
            if tau + 2 < NTAU:
                emit_A(tau + 2, defer=True)
            emit_G_evac(tau, (1,))

        C("gpsimd", lambda e: None, waits=[S["store_prev"]])

        @block.sync
        def _(eng):
            for f in ctx.progs["sync"]:
                f(eng, sems)

        @block.vector
        def _(eng):
            for f in ctx.progs["vector"]:
                f(eng, sems)

        @block.scalar
        def _(eng):
            for f in ctx.progs["scalar"]:
                f(eng, sems)

        @block.tensor
        def _(eng):
            for f in ctx.progs["tensor"]:
                f(eng, sems)

        @block.gpsimd
        def _(eng):
            for f in ctx.progs["gpsimd"]:
                f(eng, sems)

    return nc


_NC_CACHE = {}


def kernel(L0_attn_x, L0_attn_W, L0_attn_b, L0_attn_gamma, L0_attn_beta,
           L0_ffn_weight, L0_ffn_bias):
    x = np.ascontiguousarray(np.asarray(L0_attn_x, dtype=np.float32))
    W = np.ascontiguousarray(np.asarray(L0_attn_W, dtype=np.float32))
    b = np.ascontiguousarray(np.asarray(L0_attn_b, dtype=np.float32))
    gamma = np.ascontiguousarray(np.asarray(L0_attn_gamma, dtype=np.float32))
    beta = np.ascontiguousarray(np.asarray(L0_attn_beta, dtype=np.float32))
    wf = np.ascontiguousarray(np.asarray(L0_ffn_weight, dtype=np.float32))
    bf_ = np.ascontiguousarray(np.asarray(L0_ffn_bias, dtype=np.float32))
    iden = np.eye(128, dtype=np.float32)

    if "nc" not in _NC_CACHE:
        _NC_CACHE["nc"] = build_nc()
    nc = _NC_CACHE["nc"]
    in_maps = [{"x": x[i], "W": W, "b": b, "gamma": gamma, "beta": beta,
                "wf": wf, "bf": bf_, "iden": iden} for i in range(8)]
    res = run_bass_kernel_spmd(nc, in_maps, core_ids=list(range(8)))
    return np.stack([np.asarray(r["out"]) for r in res.results], axis=0)


# version 39
# speedup vs baseline: 1.3252x; 1.0035x over previous
import contextlib
import numpy as np
import concourse.bass as bass
import concourse.mybir as mybir
from concourse.bass_utils import run_bass_kernel_spmd

F32 = mybir.dt.float32
BF16 = mybir.dt.bfloat16
FP8 = mybir.dt.float8e4
Alu = mybir.AluOpType
Act = mybir.ActivationFunctionType
AX = mybir.AxisListType

T = 2048
D = 1024
TT = 256
NTAU = T // TT
NI = TT // 128
KC = 8
E_LN = 1e-5
E_ATT = 1e-6
N_LD = 8
N_TR = 10
N_ST = 4


class Ctx:
    def __init__(self):
        self.cnt = {}
        self.progs = {"sync": [], "vector": [], "scalar": [], "tensor": [], "gpsimd": []}

    def op(self, engine, fn, waits=(), inc=None, drain=False):
        after = None
        if inc is not None:
            sem, amt = inc
            self.cnt[sem] = self.cnt.get(sem, 0) + amt
            after = self.cnt[sem]
        waits = tuple((s, v) for (s, v) in waits if s is not None and v > 0)

        def emit(eng, sems, fn=fn, waits=waits, inc=inc, drain=drain):
            for s, v in waits:
                eng.wait_ge(sems[s], v)
            instr = fn(eng)
            if inc is not None:
                if instr is None:
                    instr = eng.nop()
                instr.then_inc(sems[inc[0]], inc[1])
            if drain:
                eng.drain()
        self.progs[engine].append(emit)
        return after


class DmaPool:
    def __init__(self, ctx, name, nslots):
        self.ctx, self.name, self.n = ctx, name, nslots
        self.uses = [0] * nslots
        self.rr = 0

    def dma(self, queue, fn, waits=()):
        s = self.rr
        self.rr = (self.rr + 1) % self.n
        semname = "%s%d" % (self.name, s)
        w = list(waits)
        if self.uses[s] > 0:
            w.append((semname, 16 * self.uses[s]))
        self.uses[s] += 1
        val = self.ctx.op(queue, fn, waits=w, inc=(semname, 16))
        return (semname, val)


NONE_W = ("sv", 0)


def build_nc():
    nc = bass.Bass("TRN2", target_bir_lowering=False)
    x_h = nc.dram_tensor("x", [T, D], F32, kind="ExternalInput").ap()
    W_h = nc.dram_tensor("W", [D, 2 * D], F32, kind="ExternalInput").ap()
    b_h = nc.dram_tensor("b", [2 * D], F32, kind="ExternalInput").ap()
    gamma_h = nc.dram_tensor("gamma", [D], F32, kind="ExternalInput").ap()
    beta_h = nc.dram_tensor("beta", [D], F32, kind="ExternalInput").ap()
    wf_h = nc.dram_tensor("wf", [D, D], F32, kind="ExternalInput").ap()
    bf_h = nc.dram_tensor("bf", [D], F32, kind="ExternalInput").ap()
    iden_h = nc.dram_tensor("iden", [128, 128], F32, kind="ExternalInput").ap()
    out_h = nc.dram_tensor("out", [T, D], F32, kind="ExternalOutput").ap()

    ctx = Ctx()
    C = ctx.op
    LD = DmaPool(ctx, "ld", N_LD)
    TR = DmaPool(ctx, "tr", N_TR)
    STP = DmaPool(ctx, "st", N_ST)

    es = contextlib.ExitStack()
    with es:
        Wb = es.enter_context(nc.sbuf_tensor("Wb", [128, KC * 2048], BF16))
        wdT = es.enter_context(nc.sbuf_tensor("wdT", [128, KC * 1024], BF16))
        gb = es.enter_context(nc.sbuf_tensor("gb", [128, D], F32))
        bb = es.enter_context(nc.sbuf_tensor("bb", [128, D], F32))
        ident = es.enter_context(nc.sbuf_tensor("ident", [128, 128], F32))
        identb = es.enter_context(nc.sbuf_tensor("identb", [128, 128], BF16))
        ones1 = es.enter_context(nc.sbuf_tensor("ones1", [1, 128], F32))
        grow = es.enter_context(nc.sbuf_tensor("grow", [1, 2 * D], F32))
        bfrow = es.enter_context(nc.sbuf_tensor("bfrow", [1, D], F32))
        bfh = es.enter_context(nc.sbuf_tensor("bfh", [1, D], F32))
        bvec = es.enter_context(nc.sbuf_tensor("bvec", [128, 16], F32))
        negb = es.enter_context(nc.sbuf_tensor("negb", [128, 16], F32))
        b05 = es.enter_context(nc.sbuf_tensor("b05", [128, 16], F32))
        epsln = es.enter_context(nc.sbuf_tensor("epsln", [128, 1], F32))
        epsat = es.enter_context(nc.sbuf_tensor("epsat", [128, 1], F32))
        Wst = es.enter_context(nc.sbuf_tensor("Wst", [128, 2048], F32))
        wfst = es.enter_context(nc.sbuf_tensor("wfst", [128, 1024], F32))
        wq8 = es.enter_context(nc.sbuf_tensor("wq8", [128, 1024], FP8))
        wdtok = es.enter_context(nc.sbuf_tensor("wdtok", [128, 1024], BF16))
        wsc = es.enter_context(nc.sbuf_tensor("wsc", [128, 128], F32))
        xt = es.enter_context(nc.sbuf_tensor("xt", [128, NI * D], F32))
        xn1 = es.enter_context(nc.sbuf_tensor("xn1", [128, 2 * D], F32))
        xn2 = es.enter_context(nc.sbuf_tensor("xn2", [128, 2 * D], F32))
        q81 = es.enter_context(nc.sbuf_tensor("q81", [128, 2 * D], FP8))
        q82 = q81
        xdt1 = es.enter_context(nc.sbuf_tensor("xdt1", [128, 2 * D], BF16))
        xdt2 = es.enter_context(nc.sbuf_tensor("xdt2", [128, 2 * D], BF16))
        xdf1 = es.enter_context(nc.sbuf_tensor("xdf1", [128, KC * TT], BF16))
        xdf2 = es.enter_context(nc.sbuf_tensor("xdf2", [128, KC * TT], BF16))
        yt = es.enter_context(nc.sbuf_tensor("yt", [128, NI * D], F32))
        y2t = Wst[:, 0:2 * D]
        t1t = wfst[:, 0:D]
        bby = es.enter_context(nc.sbuf_tensor("bby", [128, 2 * NI * D], F32))
        l2b = es.enter_context(nc.sbuf_tensor("l2b", [128, 2 * 2 * D], BF16))
        l2f = es.enter_context(nc.sbuf_tensor("l2f", [128, KC * TT], BF16))
        cfull = es.enter_context(nc.sbuf_tensor("cfull", [128, KC * TT], F32))
        vfull = es.enter_context(nc.sbuf_tensor("vfull", [128, KC * TT], F32))
        gsc = es.enter_context(nc.sbuf_tensor("gsc", [128, 2 * TT], F32))
        sgf = es.enter_context(nc.sbuf_tensor("sgf", [128, 2 * TT], F32))
        zsc = wdtok[:].bitcast(F32)
        hf = es.enter_context(nc.sbuf_tensor("hf", [128, KC * TT], F32))
        carry = es.enter_context(nc.sbuf_tensor("carry", [128, KC], F32))
        nt = es.enter_context(nc.sbuf_tensor("nt", [128, D], F32))
        outt = es.enter_context(nc.sbuf_tensor("outt", [128, D], F32))
        bns = es.enter_context(nc.sbuf_tensor("bns", [128, 72], F32))
        mv = es.enter_context(nc.sbuf_tensor("mv", [128, 12], F32))
        sden = es.enter_context(nc.sbuf_tensor("sden", [128, 6], F32))
        rr = es.enter_context(nc.sbuf_tensor("rr", [128, 6], F32))
        mb = es.enter_context(nc.sbuf_tensor("mb", [128, 6], F32))
        qsc = es.enter_context(nc.sbuf_tensor("qsc", [128, 512], F32))
        dmy = es.enter_context(nc.sbuf_tensor("dmy", [128, 1], F32))
        psF = es.enter_context(nc.psum_tensor("psF", [128, 1024], F32))
        psG = es.enter_context(nc.psum_tensor("psG", [128, 1024], F32))
        psE = es.enter_context(nc.psum_tensor("psE", [128, 1024], F32))
        psP0 = es.enter_context(nc.psum_tensor("psP0", [128, 512], F32))
        psP1 = es.enter_context(nc.psum_tensor("psP1", [128, 512], F32))
        sems = {}
        for nm in (["sv", "ss", "sten", "sg"]
                   + ["ld%d" % i for i in range(N_LD)]
                   + ["tr%d" % i for i in range(N_TR)]
                   + ["st%d" % i for i in range(N_ST)]):
            sems[nm] = es.enter_context(nc.semaphore(nm))
        block = es.enter_context(nc.Block())
        psP = [psP0, psP1]

        ld_iden = LD.dma("sync", lambda e: e.dma_start(out=ident[:], in_=iden_h))
        ld_grow = LD.dma("sync", lambda e: e.dma_start(out=grow[:, 0:D], in_=gamma_h.unsqueeze(0)))
        ld_brow = LD.dma("sync", lambda e: e.dma_start(out=grow[:, D:2 * D], in_=beta_h.unsqueeze(0)))
        ld_bfrow = LD.dma("sync", lambda e: e.dma_start(out=bfrow[:], in_=bf_h.unsqueeze(0)))
        ld_bvec = LD.dma("sync", lambda e: e.dma_start(
            out=bvec[:], in_=b_h.rearrange("(j p) -> p j", p=128),
            allow_slow_non_contiguous=True))

        C("vector", lambda e: e.memset(epsln[:], E_LN))
        C("vector", lambda e: e.memset(epsat[:], E_ATT))
        C("vector", lambda e: e.memset(ones1[:], 1.0))
        sv_ones = C("vector", lambda e: e.tensor_copy(out=identb[:], in_=ident[:]),
                    waits=[ld_iden], inc=("sv", 1))

        prev_q = 0
        prev_wtr = NONE_W
        wfq_done = {}
        init_ld_x0 = None
        for k in range(KC):
            if k == 2:
                init_ld_x0 = LD.dma("scalar", lambda e: e.dma_start(
                    out=xt[:].rearrange("p (i c) -> p i c", i=NI),
                    in_=x_h[0:TT, :].rearrange("(i p) c -> p i c", p=128)))
            wstg = wfst[:, 0:D] if k % 2 == 0 else xn2[:, 0:D]
            ldwf = LD.dma("scalar", lambda e, k=k, wstg=wstg: e.dma_start(
                out=wstg, in_=wf_h[k * 128:(k + 1) * 128, :]),
                waits=[("sv", wfq_done.get(k - 2, 0))])
            wam, wscl, wi6, ws2 = (wsc[:, 0:32], wsc[:, 32:64], wsc[:, 64:96], wsc[:, 96:128])
            C("vector", lambda e, wstg=wstg: e.tensor_reduce(
                out=wam, in_=wstg.rearrange("p (b k) -> p b k", k=32),
                axis=AX.X, op=Alu.max, apply_absolute_value=True),
              waits=[ldwf], drain=True)
            C("vector", lambda e: e.tensor_scalar(
                out=wscl, in0=wam, scalar1=1.0 / 7.5, scalar2=1e-30,
                op0=Alu.mult, op1=Alu.max), drain=True)
            C("vector", lambda e: e.reciprocal(out=wi6, in_=wscl), drain=True)
            C("vector", lambda e: e.tensor_scalar(
                out=wi6, in0=wi6, scalar1=2.0 ** -6, scalar2=None, op0=Alu.mult), drain=True)
            C("vector", lambda e: e.tensor_scalar(
                out=ws2, in0=wscl, scalar1=32.0, scalar2=None, op0=Alu.mult), drain=True)
            wfq_done[k] = C("vector", lambda e, wstg=wstg: e.tensor_tensor(
                out=wq8[:].rearrange("p (b k) -> p b k", k=32),
                in0=wstg.rearrange("p (b k) -> p b k", k=32),
                in1=wi6.unsqueeze(-1).broadcast_to([128, 32, 32]), op=Alu.mult),
                inc=("sv", 1), drain=True)
            prev_q = C("vector", lambda e: e.tensor_tensor(
                out=wdtok[:].rearrange("p (b k) -> p b k", k=32),
                in0=wq8[:].rearrange("p (b k) -> p b k", k=32),
                in1=ws2.unsqueeze(-1).broadcast_to([128, 32, 32]), op=Alu.mult),
                waits=[prev_wtr], inc=("sv", 1), drain=True)
            prev_wtr = TR.dma("sync", lambda e, k=k: e.dma_start(
                out=wdT[:].rearrange("p (m c l) -> p m c l", m=KC, c=KC)[:, :, k, :],
                in_=wdtok[:], transpose=True),
                waits=[("sv", prev_q)])
        tr_wdT = prev_wtr

        C("vector", lambda e: e.tensor_scalar(out=negb[:], in0=bvec[:], scalar1=-1.0,
                                              scalar2=None, op0=Alu.mult),
          waits=[ld_bvec])
        sv_bvecs = C("vector", lambda e: e.tensor_scalar(out=b05[:], in0=bvec[:], scalar1=0.5,
                                                         scalar2=None, op0=Alu.add),
                     inc=("sv", 1))
        sv_bfh = C("vector", lambda e: e.tensor_scalar(out=bfh[:], in0=bfrow[:], scalar1=0.5,
                                                       scalar2=None, op0=Alu.mult),
                   waits=[ld_bfrow], inc=("sv", 1), drain=True)

        st_gbmm = 0
        for half in range(2):
            C("tensor", lambda e, h=half: e.matmul(
                out=psF[:, h * 512:(h + 1) * 512], lhsT=ones1[:],
                rhs=grow[:, h * 512:(h + 1) * 512], start=True, stop=True),
              waits=([ld_grow, ld_iden, ld_brow, ("sv", sv_ones)] if half == 0 else ()))
            st_gbmm = C("tensor", lambda e, h=half: e.matmul(
                out=psG[:, h * 512:(h + 1) * 512], lhsT=ones1[:],
                rhs=grow[:, D + h * 512:D + (h + 1) * 512], start=True, stop=True),
                inc=(("sten", 1) if half == 1 else None)) or st_gbmm
        C("vector", lambda e: e.tensor_copy(out=gb[:], in_=psF[:]),
          waits=[("sten", st_gbmm)])
        sv_gb = C("vector", lambda e: e.tensor_copy(out=bb[:], in_=psG[:]),
                  inc=("sv", 1), drain=True)

        def ln_stats2(src2_getter, base_slot, eps_tile, wait_list, filler=None):
            s6a = bns[:, base_slot * 12:base_slot * 12 + 12]
            s6b = bns[:, (base_slot + 1) * 12:(base_slot + 1) * 12 + 12]
            C("vector", lambda e: e.bn_stats(out=s6a[:, 0:6], in_=src2_getter()[:, 0:512]),
              waits=wait_list)
            C("vector", lambda e: e.bn_stats(out=s6a[:, 6:12], in_=src2_getter()[:, 512:1024]))
            C("vector", lambda e: e.bn_stats(out=s6b[:, 0:6], in_=src2_getter()[:, 1024:1536]))
            C("vector", lambda e: e.bn_stats(out=s6b[:, 6:12], in_=src2_getter()[:, 1536:2048]),
              drain=True)
            mv2 = mv[:, base_slot * 2:base_slot * 2 + 4]
            C("vector", lambda e: e.bn_aggr(out=mv2[:, 0:2], in_=s6a))
            sv_mv = C("vector", lambda e: e.bn_aggr(out=mv2[:, 2:4], in_=s6b),
                      inc=("sv", 1))
            sd = sden[:, base_slot:base_slot + 2]
            varap = mv2.rearrange("p (s c) -> p s c", c=2)[:, :, 1:2].squeeze(-1)
            ss_sq = C("scalar", lambda e: e.activation(
                out=sd, in_=varap, func=Act.Sqrt, bias=eps_tile[:], scale=1.0),
                waits=[("sv", sv_mv)], inc=("ss", 1))
            if filler is not None:
                filler()
            r2ap = rr[:, base_slot:base_slot + 2]
            sv_r = C("vector", lambda e: e.reciprocal(out=r2ap, in_=sd),
                     waits=[("ss", ss_sq)], inc=("sv", 1), drain=True)
            rs = [rr[:, base_slot + i:base_slot + 1 + i] for i in range(2)]
            mus = [mv[:, base_slot * 2 + 2 * i:base_slot * 2 + 2 * i + 1] for i in range(2)]
            return sv_r, rs, mus

        def mkmb(base_slot, sv_r):
            mv2mu = mv[:, base_slot * 2:base_slot * 2 + 4].rearrange(
                "p (s c) -> p s c", c=2)[:, :, 0:1].squeeze(-1)
            m2ap = mb[:, base_slot:base_slot + 2]
            sv_m = C("vector", lambda e: e.scalar_tensor_tensor(
                out=m2ap, in0=mv2mu, scalar=-1.0, in1=rr[:, base_slot:base_slot + 2],
                op0=Alu.mult, op1=Alu.mult), inc=("sv", 1))
            return sv_m, [mb[:, base_slot + i:base_slot + 1 + i] for i in range(2)]

        S = dict(
            psF_free=(sv_gb, 0), psG_free=(sv_gb, 0), psE_free=(0, 0),
            psPE_free=(0, 0),
            sv_x_consumed=0, sten_hT=0, sten_ffn1=0, sten_ffn2=0, sten_proj=0,
            tr_xd1_prev=NONE_W, tr_xd2_prev=NONE_W,
            sv_xn1_free=0, sv_xn2_free=wfq_done[KC - 1],
            prev_scan=0, sv_carry=0, gscf={0: 0, 1: 0}, sgff={0: 0, 1: 0},
            sg_bby={}, sv_y={}, sv_y2=0, sv_y2_hist={},
            ld_x={}, tr_xdf={}, tr_xd2={}, tr_l2f={}, store_prev=NONE_W,
        )
        S["ld_x"][(0, 0)] = init_ld_x0
        S["ld_x"][(0, 1)] = init_ld_x0

        def emit_A_load(tau):
            r0 = tau * TT
            ld = LD.dma("scalar", lambda e, r0=r0: e.dma_start(
                out=xt[:].rearrange("p (i c) -> p i c", i=NI),
                in_=x_h[r0:r0 + TT, :].rearrange("(i p) c -> p i c", p=128)),
                waits=[("sv", S["sv_x_consumed"])])
            for i in range(NI):
                S["ld_x"][(tau, i)] = ld

        def emit_A(tau, defer=False):
            svr, r1s, mu1s = ln_stats2(lambda: xt[:], 0, epsln,
                                       [S["ld_x"][(tau, 0)], S["ld_x"][(tau, 1)]])
            for i in range(NI):
                C("vector", lambda e, i=i, r=r1s[i], m=mu1s[i]: e.tensor_scalar(
                    out=xn1[:, i * D:(i + 1) * D], in0=xt[:, i * D:(i + 1) * D],
                    scalar1=m, scalar2=r, op0=Alu.subtract, op1=Alu.mult),
                    waits=[("sv", max(svr, S["sv_xn1_free"]))], drain=(i == NI - 1))
            C("scalar", lambda e: e.activation(
                out=dmy[:], in_=epsln[:], func=Act.Sigmoid, scale=1.0, bias=0.0))
            qa = qsc[:, 0:64]
            qiv = qsc[:, 64:128]
            qi6 = qsc[:, 128:192]
            qs2 = qsc[:, 192:256]

            def th_amax():
                C("vector", lambda e: e.tensor_reduce(
                    out=qa, in_=xn1[:].rearrange("p (b k) -> p b k", k=32),
                    axis=AX.X, op=Alu.max, apply_absolute_value=True),
                  waits=[S["tr_xd1_prev"]], drain=True)

            def th_scales():
                C("vector", lambda e: e.reciprocal(out=qiv, in_=qa), drain=True)
                C("vector", lambda e: e.tensor_scalar(
                    out=qi6, in0=qiv, scalar1=7.5 * 2.0 ** -6, scalar2=(1e30 * 2.0 ** -6),
                    op0=Alu.mult, op1=Alu.min))
                C("vector", lambda e: e.tensor_scalar(
                    out=qs2, in0=qa, scalar1=64.0 / 7.5, scalar2=6.4e-29,
                    op0=Alu.mult, op1=Alu.max), drain=True)

            def th_q8():
                S["_svq8"] = C("vector", lambda e: e.tensor_tensor(
                    out=q81[:].rearrange("p (b k) -> p b k", k=32),
                    in0=xn1[:].rearrange("p (b k) -> p b k", k=32),
                    in1=qi6.unsqueeze(-1).broadcast_to([128, 64, 32]),
                    op=Alu.mult), inc=("sv", 1), drain=True)

            def mk_th_xd(i):
                def th():
                    hi = slice(i * 32, (i + 1) * 32)
                    sv_xd = C("vector", lambda e, i=i, hi=hi: e.tensor_tensor(
                        out=xdt1[:, i * D:(i + 1) * D].rearrange("p (b k) -> p b k", k=32),
                        in0=q81[:, i * D:(i + 1) * D].rearrange("p (b k) -> p b k", k=32),
                        in1=qs2[:, hi].unsqueeze(-1).broadcast_to([128, 32, 32]),
                        op=Alu.mult), inc=("sv", 1), drain=True)
                    for hh in range(2):
                        S["tr_xdf"][(tau, i, hh)] = TR.dma("sync", lambda e, i=i, hh=hh: e.dma_start(
                            out=xdf1[:].rearrange("p (k t) -> p k t", k=KC)[
                                :, hh * 4:(hh + 1) * 4, i * 128:(i + 1) * 128],
                            in_=xdt1[:, i * D + hh * 512:i * D + (hh + 1) * 512], transpose=True),
                            waits=[("sv", sv_xd), ("sten", S["sten_ffn1"])])
                    if i == NI - 1:
                        S["sv_xn1_free"] = S["_svq8"]
                        S["tr_xd1_prev"] = S["tr_xdf"][(tau, NI - 1, 1)]
                return th

            ths = [th_amax, th_scales, th_q8, mk_th_xd(0), mk_th_xd(1)]
            if defer:
                S["a_ths"] = ths
            else:
                for t in ths:
                    t()

        def emit_B_pe(tau):
            st_mmB = {}
            for i in range(NI):
                pbk = psF if i == 0 else psE
                freek = "psF_free" if i == 0 else "psE_free"
                mm_wait0 = [S["tr_xdf"][(tau, i, 0)], S["ld_x"][(tau, i)]]
                mm_wait1 = [S["tr_xdf"][(tau, i, 1)]]
                if tau == 0 and i == 0:
                    mm_wait0.append(tr_wdT)
                for half in range(2):
                    C("tensor", lambda e, half=half, pbk=pbk: e.matmul(
                        out=pbk[:, half * 512:(half + 1) * 512], lhsT=ones1[:],
                        rhs=bfh[:, half * 512:(half + 1) * 512], start=True, stop=False),
                      waits=[("sv", max(S[freek][0], sv_bfh)), ("ss", S[freek][1])]
                      if half == 0 else ())
                for half in range(2):
                    for k in range(KC):
                        st_mmB[i] = C("tensor", lambda e, i=i, half=half, k=k, pbk=pbk: e.matmul(
                            out=pbk[:, half * 512:(half + 1) * 512],
                            lhsT=xdf1[:, k * TT + i * 128:k * TT + (i + 1) * 128],
                            rhs=wdT[:, k * 1024 + half * 512:k * 1024 + (half + 1) * 512],
                            start=False, stop=(k == KC - 1)),
                          waits=(mm_wait0 if (half == 0 and k == 0)
                                 else (mm_wait1 if (half == 0 and k == 4) else ())),
                          inc=(("sten", 1) if (half == 1 and k == KC - 1) else None))
            S["st_mmB"] = st_mmB
            S["sten_ffn1"] = st_mmB[NI - 1]

        def emit_B_evac(tau):
            sv_y = 0
            sg_prev_bby = S["sg_bby"].get(tau - 1, 0)
            for i in range(NI):
                pbk = psF if i == 0 else psE
                freek = "psF_free" if i == 0 else "psE_free"
                sv_y = C("vector", lambda e, i=i, pbk=pbk: e.tensor_tensor(
                    out=yt[:, i * D:(i + 1) * D], in0=pbk[:],
                    in1=xt[:, i * D:(i + 1) * D], op=Alu.add),
                    waits=[("sten", S["st_mmB"][i]), ("sg", sg_prev_bby)], inc=("sv", 1))
                S[freek] = (sv_y, 0)
                C("gpsimd", lambda e, i=i, par=tau % 2: e.tensor_tensor(
                    out=bby[:, (par * NI + i) * D:(par * NI + i + 1) * D],
                    in0=yt[:, i * D:(i + 1) * D], in1=bb[:], op=Alu.add),
                    waits=[("sv", max(sv_y, sv_gb, S["sv_y2"]))], inc=("sg", 1), drain=True)
            S["sv_x_consumed"] = sv_y
            S["sv_y"][tau] = sv_y
            S["sg_bby"][tau] = ctx.cnt["sg"]

        def emit_C(tau, filler=None):
            par = tau % 2
            C("vector", lambda e: e.drain())
            svr2, r2s, mu2s = ln_stats2(lambda: yt[:], 2, epsln, [("sv", S["sv_y"][tau])],
                                        filler=filler)
            for i in range(NI):
                sv_l2b = C("vector", lambda e, i=i, par=par, m=mu2s[i], r=r2s[i]: e.tensor_scalar(
                    out=l2b[:, (par * 2 + i) * D:(par * 2 + i + 1) * D],
                    in0=yt[:, i * D:(i + 1) * D],
                    scalar1=m, scalar2=r, op0=Alu.subtract, op1=Alu.mult),
                    waits=[("sv", svr2), ("sten", S["sten_hT"])], inc=("sv", 1))
                S["tr_l2f"][(tau, i)] = TR.dma("sync", lambda e, i=i, par=par: e.dma_start(
                    out=l2f[:].rearrange("p (k t) -> p k t", k=KC)[:, :, i * 128:(i + 1) * 128],
                    in_=l2b[:, (par * 2 + i) * D:(par * 2 + i + 1) * D], transpose=True),
                    waits=[("sv", sv_l2b), ("sten", S["sten_proj"])])

        def emit_D(tau):
            st_proj = 0
            scan_prev = dict(S.get("scan_inc", {}))
            S["scan_inc"] = {}
            a_ths = S.pop("a_ths", [])
            if a_ths:
                a_ths[0]()
            for jp in range(KC):
                odd = jp % 2
                if odd:
                    pbs = (psF[:, 0:256], psF[:, 512:768])
                    freev = S["psF_free"]
                else:
                    pbs = (psP0[:, 0:256], psP1[:, 0:256])
                    freev = S["psPE_free"]
                st_pj = 0
                st_g1 = 0
                for which, jj in enumerate((jp, jp + 8)):
                    pb = pbs[which]
                    w0 = [S["tr_l2f"][(tau, 0)], S["tr_l2f"][(tau, 1)],
                          ("sv", freev[0]), ("ss", freev[1])]
                    if odd:
                        w0.append(("sten", S["sten_ffn1"]))
                    if tau == 0 and jp == 0 and which == 0:
                        w0.append(("sv", sv_Wb))
                    for k in range(KC):
                        st_pj = C("tensor", lambda e, jj=jj, k=k, pb=pb: e.matmul(
                            out=pb,
                            lhsT=Wb[:, k * 2048 + jj * 128:k * 2048 + (jj + 1) * 128],
                            rhs=l2f[:, k * TT:(k + 1) * TT],
                            start=(k == 0), stop=(k == KC - 1)),
                          waits=w0 if k == 0 else (),
                          inc=(("sten", 1) if k == KC - 1 else None))
                    if which == 0:
                        st_g1 = st_pj
                st_proj = st_pj
                sl = jp % 2
                ss_sig = C("scalar", lambda e, jp=jp, sl=sl, pb=pbs[0]: e.activation(
                    out=sgf[:, sl * TT:(sl + 1) * TT], in_=pb,
                    func=Act.Sigmoid, scale=1.0, bias=bvec[:, jp:jp + 1]),
                    waits=[("sten", st_g1), ("sv", S["sgff"][sl])], inc=("ss", 1))
                ss_c = C("scalar", lambda e, jp=jp, pb=pbs[1]: e.activation(
                    out=cfull[:, jp * TT:(jp + 1) * TT], in_=pb,
                    func=Act.Sigmoid, scale=-1.0, bias=negb[:, jp + 8:jp + 9]),
                    waits=[("sten", st_pj), ("sv", scan_prev.get(jp, 0))], inc=("ss", 1))
                sv_g = C("vector", lambda e, jp=jp, sl=sl, pb=pbs[0]: e.scalar_tensor_tensor(
                    out=gsc[:, sl * TT:(sl + 1) * TT], in0=pb,
                    scalar=b05[:, jp:jp + 1], in1=sgf[:, sl * TT:(sl + 1) * TT],
                    op0=Alu.add, op1=Alu.max),
                    waits=[("ss", ss_sig)], inc=("sv", 1))
                C("vector", lambda e, jp=jp, sl=sl: e.tensor_scalar(
                    out=zsc[:, sl * TT:(sl + 1) * TT],
                    in0=cfull[:, jp * TT:(jp + 1) * TT],
                    scalar1=-1.0, scalar2=1.0, op0=Alu.mult, op1=Alu.add),
                  waits=[("ss", ss_c)], drain=True)
                sv_v = C("vector", lambda e, jp=jp, sl=sl: e.tensor_tensor(
                    out=vfull[:, jp * TT:(jp + 1) * TT],
                    in0=zsc[:, sl * TT:(sl + 1) * TT],
                    in1=gsc[:, sl * TT:(sl + 1) * TT], op=Alu.mult),
                    inc=("sv", 1), drain=True)
                init = 0.0 if tau == 0 else carry[:, jp:jp + 1]
                S["prev_scan"] = C("vector", lambda e, jp=jp, init=init: e.tensor_tensor_scan(
                    out=hf[:, jp * TT:(jp + 1) * TT],
                    data0=cfull[:, jp * TT:(jp + 1) * TT],
                    data1=vfull[:, jp * TT:(jp + 1) * TT], initial=init,
                    op0=Alu.mult, op1=Alu.add),
                    waits=([("sten", S["sten_hT"])] if jp == 0 else ()),
                    inc=("sv", 1), drain=(jp == KC - 1))
                S["scan_inc"][jp] = S["prev_scan"]
                S["sv_v"] = sv_v
                S["sgff"][sl] = sv_g
                if odd:
                    S["psF_free"] = (sv_v, ss_sig)
                else:
                    S["psPE_free"] = (sv_v, ss_sig)
                if jp % 2 == 1 and (jp + 1) // 2 < len(a_ths):
                    a_ths[(jp + 1) // 2]()
            S["sten_proj"] = st_proj
            C("scalar", lambda e: e.activation(
                out=dmy[:], in_=epsln[:], func=Act.Sqrt, scale=1.0, bias=epsln[:]))
            def carry_th():
                S["sv_carry"] = C("vector", lambda e: e.tensor_copy(
                    out=carry[:],
                    in_=hf[:].rearrange("p (k t) -> p k t", k=KC)[:, :, TT - 1:TT].squeeze(-1)),
                    inc=("sv", 1), drain=True)
            S["carry_th"] = carry_th

        def emit_E(tau):
            par = tau % 2
            st_hT = 0
            sv_y2 = 0
            for i in range(NI):
                pbk = psE if i == 0 else psG
                freek = "psE_free" if i == 0 else "psG_free"
                w0 = [("sv", S[freek][0]), ("ss", S[freek][1])]
                for half in range(2):
                    C("tensor", lambda e, i=i, half=half, par=par, pbk=pbk: e.matmul(
                        out=pbk[:, half * 512:(half + 1) * 512], lhsT=identb[:],
                        rhs=l2b[:, (par * 2 + i) * D + half * 512:
                                (par * 2 + i) * D + (half + 1) * 512],
                        start=True, stop=False, skip_group_check=True),
                      waits=w0 if half == 0 else ())
                for j in range(KC):
                    st_hT = C("tensor", lambda e, i=i, j=j, pbk=pbk: e.matmul(
                        out=pbk[:, j * 128:(j + 1) * 128],
                        lhsT=hf[:, j * TT + i * 128:j * TT + (i + 1) * 128],
                        rhs=ident[:], is_transpose=True,
                        start=False, stop=(j == KC - 1), skip_group_check=True),
                      waits=([("sv", S["scan_inc"][j])] if (i == 0 or j == 0) else ()),
                      inc=(("sten", 1) if j == KC - 1 else None)) or st_hT
                s6 = bns[:, (4 + i) * 12:(4 + i) * 12 + 12]
                C("vector", lambda e, s6=s6, pbk=pbk: e.bn_stats(out=s6[:, 0:6], in_=pbk[:, 0:512]),
                  waits=[("sten", st_hT)])
                C("vector", lambda e, s6=s6, pbk=pbk: e.bn_stats(out=s6[:, 6:12], in_=pbk[:, 512:1024]),
                  drain=True)
                mvs = mv[:, (4 + i) * 2:(4 + i) * 2 + 2]
                sv_mv = C("vector", lambda e, s6=s6, mvs=mvs: e.bn_aggr(out=mvs, in_=s6),
                          inc=("sv", 1))
                sd = sden[:, 4 + i:5 + i]
                ss_sq = C("scalar", lambda e, sd=sd, mvs=mvs: e.activation(
                    out=sd, in_=mvs[:, 1:2], func=Act.Sqrt, bias=epsat[:], scale=1.0),
                    waits=[("sv", sv_mv)], inc=("ss", 1))
                r3 = rr[:, 4 + i:5 + i]
                C("vector", lambda e, r3=r3, sd=sd: e.reciprocal(out=r3, in_=sd),
                  waits=[("ss", ss_sq)], drain=True)
                sv_n = C("vector", lambda e, mvs=mvs, r3=r3, pbk=pbk: e.tensor_scalar(
                    out=nt[:, 0:D], in0=pbk[:],
                    scalar1=mvs[:, 0:1], scalar2=r3, op0=Alu.subtract, op1=Alu.mult),
                    inc=("sv", 1), drain=True)
                S[freek] = (sv_n, 0)
                C("vector", lambda e: e.tensor_tensor(
                    out=t1t, in0=nt[:, 0:D], in1=gb[:], op=Alu.mult),
                  waits=[("sg", S["sg_bby"][tau])], drain=True)
                sv_y2 = C("vector", lambda e, i=i, par=par: e.tensor_tensor(
                    out=y2t[:, i * D:(i + 1) * D], in0=t1t,
                    in1=bby[:, (par * NI + i) * D:(par * NI + i + 1) * D], op=Alu.add),
                    inc=("sv", 1))
            S["sten_hT"] = st_hT
            S["sv_y2"] = sv_y2
            S["sv_y2_hist"][tau] = sv_y2

        def emit_F(tau):
            C("vector", lambda e: e.drain())
            svr4, r4s, mu4s = ln_stats2(lambda: y2t[:], 2, epsln, [("sv", S["sv_y2_hist"][tau])],
                                        filler=S.pop("carry_th", None))
            for i in range(NI):
                C("vector", lambda e, i=i, r=r4s[i], m=mu4s[i]: e.tensor_scalar(
                    out=xn2[:, i * D:(i + 1) * D], in0=y2t[:, i * D:(i + 1) * D],
                    scalar1=m, scalar2=r, op0=Alu.subtract, op1=Alu.mult),
                    waits=[("sv", max(svr4, S["sv_xn2_free"]))], drain=(i == NI - 1))
            qa2 = qsc[:, 256 + 0:256 + 64]
            qiv2 = qsc[:, 256 + 64:256 + 128]
            qi62 = qsc[:, 256 + 128:256 + 192]
            qs22 = qsc[:, 256 + 192:256 + 256]
            C("vector", lambda e: e.tensor_reduce(
                out=qa2, in_=xn2[:].rearrange("p (b k) -> p b k", k=32),
                axis=AX.X, op=Alu.max, apply_absolute_value=True),
              waits=[S["tr_xd2_prev"]], drain=True)
            C("vector", lambda e: e.reciprocal(out=qiv2, in_=qa2), drain=True)
            C("vector", lambda e: e.tensor_scalar(
                out=qi62, in0=qiv2, scalar1=7.5 * 2.0 ** -6, scalar2=(1e30 * 2.0 ** -6),
                op0=Alu.mult, op1=Alu.min))
            C("vector", lambda e: e.tensor_scalar(
                out=qs22, in0=qa2, scalar1=64.0 / 7.5, scalar2=6.4e-29,
                op0=Alu.mult, op1=Alu.max), drain=True)
            sv_q82 = C("vector", lambda e: e.tensor_tensor(
                out=q82[:].rearrange("p (b k) -> p b k", k=32),
                in0=xn2[:].rearrange("p (b k) -> p b k", k=32),
                in1=qi62.unsqueeze(-1).broadcast_to([128, 64, 32]),
                op=Alu.mult), inc=("sv", 1), drain=True)
            sv_xd2 = 0
            for i in range(NI):
                hi = slice(i * 32, (i + 1) * 32)
                sv_xd2 = C("vector", lambda e, i=i, hi=hi: e.tensor_tensor(
                    out=xdt2[:, i * D:(i + 1) * D].rearrange("p (b k) -> p b k", k=32),
                    in0=q82[:, i * D:(i + 1) * D].rearrange("p (b k) -> p b k", k=32),
                    in1=qs22[:, hi].unsqueeze(-1).broadcast_to([128, 32, 32]),
                    op=Alu.mult), inc=("sv", 1), drain=(i == NI - 1))
                for hh in range(2):
                    S["tr_xd2"][(tau, i, hh)] = TR.dma("sync", lambda e, i=i, hh=hh: e.dma_start(
                        out=xdf2[:].rearrange("p (k t) -> p k t", k=KC)[
                            :, hh * 4:(hh + 1) * 4, i * 128:(i + 1) * 128],
                        in_=xdt2[:, i * D + hh * 512:i * D + (hh + 1) * 512], transpose=True),
                        waits=[("sv", sv_xd2), ("sten", S["sten_ffn2"])])
            S["sv_xn2_free"] = sv_q82
            S["tr_xd2_prev"] = S["tr_xd2"][(tau, NI - 1, 1)]

        def emit_G_pe(tau):
            st_mmG = {}
            for i in range(NI):
                halves = ([psP0[:], psP1[:]] if i == 0
                          else [psG[:, 0:512], psG[:, 512:1024]])
                freek = "psPE_free" if i == 0 else "psG_free"
                mm_wait0 = [S["tr_xd2"][(tau, i, 0)]]
                mm_wait1 = [S["tr_xd2"][(tau, i, 1)]]
                for half in range(2):
                    C("tensor", lambda e, half=half, halves=halves: e.matmul(
                        out=halves[half], lhsT=ones1[:],
                        rhs=bfh[:, half * 512:(half + 1) * 512], start=True, stop=False),
                      waits=[("sv", max(S[freek][0], sv_bfh)), ("ss", S[freek][1])]
                      if half == 0 else ())
                for half in range(2):
                    for k in range(KC):
                        st_mmG[i] = C("tensor", lambda e, i=i, half=half, k=k, halves=halves: e.matmul(
                            out=halves[half],
                            lhsT=xdf2[:, k * TT + i * 128:k * TT + (i + 1) * 128],
                            rhs=wdT[:, k * 1024 + half * 512:k * 1024 + (half + 1) * 512],
                            start=False, stop=(k == KC - 1)),
                          waits=(mm_wait0 if (half == 0 and k == 0)
                                 else (mm_wait1 if (half == 0 and k == 4) else ())),
                          inc=(("sten", 1) if (half == 1 and k == KC - 1) else None))
            S["st_mmG"] = st_mmG
            S["sten_ffn2"] = st_mmG[NI - 1]

        def emit_G_evac(tau, which):
            for i in which:
                halves = ([psP0[:], psP1[:]] if i == 0
                          else [psG[:, 0:512], psG[:, 512:1024]])
                freek = "psPE_free" if i == 0 else "psG_free"
                sv_out = 0
                for half in range(2):
                    sv_out = C("vector", lambda e, i=i, half=half, halves=halves: e.tensor_tensor(
                        out=outt[:, half * 512:(half + 1) * 512], in0=halves[half],
                        in1=y2t[:, i * D + half * 512:i * D + (half + 1) * 512], op=Alu.add),
                        waits=([("sten", S["st_mmG"][i]), S["store_prev"]] if half == 0 else ()),
                        inc=("sv", 1))
                S[freek] = (sv_out, 0)
                r0 = tau * TT + i * 128
                S["store_prev"] = STP.dma("scalar", lambda e, r0=r0: e.dma_start(
                    out=out_h[r0:r0 + 128, :], in_=outt[:, 0:D]),
                    waits=[("sv", sv_out)])

        emit_A(0)
        emit_B_pe(0)
        cast_done = {}
        wstages = [Wst[:], xn1[:], yt[:], l2b[:].bitcast(F32)]
        for k in range(KC):
            stg = wstages[k % 4]
            ldWk = LD.dma("scalar", lambda e, k=k, stg=stg: e.dma_start(
                out=stg, in_=W_h[k * 128:(k + 1) * 128, :]),
                waits=[("sv", max(cast_done.get(k - 4, 0),
                                  S["sv_xn1_free"] if k % 4 == 1 else 0))])
            cast_done[k] = C("vector", lambda e, k=k, stg=stg: e.tensor_copy(
                out=Wb[:, k * 2048:(k + 1) * 2048], in_=stg),
                waits=[ldWk], inc=("sv", 1), drain=True)
        sv_Wb = cast_done[KC - 1]
        S["sv_xn1_free"] = cast_done[5]
        emit_B_evac(0)
        emit_C(0)
        emit_A_load(1)
        emit_A(1)
        for tau in range(NTAU):
            emit_D(tau)
            emit_E(tau)
            emit_F(tau)
            if tau + 1 < NTAU:
                emit_B_pe(tau + 1)
            emit_G_pe(tau)
            if tau + 1 < NTAU:
                emit_B_evac(tau + 1)
                emit_A_load(tau + 2) if tau + 2 < NTAU else None
                emit_C(tau + 1)
            emit_G_evac(tau, (0,))
            if tau + 2 < NTAU:
                emit_A(tau + 2, defer=True)
            emit_G_evac(tau, (1,))

        C("gpsimd", lambda e: None, waits=[S["store_prev"]])

        @block.sync
        def _(eng):
            for f in ctx.progs["sync"]:
                f(eng, sems)

        @block.vector
        def _(eng):
            for f in ctx.progs["vector"]:
                f(eng, sems)

        @block.scalar
        def _(eng):
            for f in ctx.progs["scalar"]:
                f(eng, sems)

        @block.tensor
        def _(eng):
            for f in ctx.progs["tensor"]:
                f(eng, sems)

        @block.gpsimd
        def _(eng):
            for f in ctx.progs["gpsimd"]:
                f(eng, sems)

    return nc


_NC_CACHE = {}


def kernel(L0_attn_x, L0_attn_W, L0_attn_b, L0_attn_gamma, L0_attn_beta,
           L0_ffn_weight, L0_ffn_bias):
    x = np.ascontiguousarray(np.asarray(L0_attn_x, dtype=np.float32))
    W = np.ascontiguousarray(np.asarray(L0_attn_W, dtype=np.float32))
    b = np.ascontiguousarray(np.asarray(L0_attn_b, dtype=np.float32))
    gamma = np.ascontiguousarray(np.asarray(L0_attn_gamma, dtype=np.float32))
    beta = np.ascontiguousarray(np.asarray(L0_attn_beta, dtype=np.float32))
    wf = np.ascontiguousarray(np.asarray(L0_ffn_weight, dtype=np.float32))
    bf_ = np.ascontiguousarray(np.asarray(L0_ffn_bias, dtype=np.float32))
    iden = np.eye(128, dtype=np.float32)

    if "nc" not in _NC_CACHE:
        _NC_CACHE["nc"] = build_nc()
    nc = _NC_CACHE["nc"]
    in_maps = [{"x": x[i], "W": W, "b": b, "gamma": gamma, "beta": beta,
                "wf": wf, "bf": bf_, "iden": iden} for i in range(8)]
    res = run_bass_kernel_spmd(nc, in_maps, core_ids=list(range(8)))
    return np.stack([np.asarray(r["out"]) for r in res.results], axis=0)
